# version 13
# baseline (speedup 1.0000x reference)
import contextlib
import numpy as np
import ml_dtypes
import concourse.bass as bass
import concourse.mybir as mybir
from concourse.bass_utils import run_bass_kernel_spmd

F32 = mybir.dt.float32
BF16 = mybir.dt.bfloat16
AF = mybir.ActivationFunctionType
ALU = mybir.AluOpType
AX = mybir.AxisListType

NT = 1024
NTILE = 8
D = 4096
DC = 32
WH = 1056
WL = 1024
COLH = 16
NSUP = 12
NGRP = 12
EPS = 1e-5
PH = 6
TAPS = []

A_XNTH = 0
A_QHT = 0
A_KHT = 16384
A_PW = 0
A_XNTL = 33792
A_CWR = 33792
A_CVT = 41984
A_WOB = 33792
A_AOT = 50176
A_EB = 32768
A_PBLO = 36864
A_PBHI = 45056
A_XA = 66560
A_XB = 74752
A_WB = 66560
A_HI = 82944
A_LO = 87040
A_VH = 82944
A_XN = 91136
A_RWTH = 93184
A_RWTL = 95232
A_I64 = 97280
A_LGT_T = 97408
A_LGT = 99456
A_ESC = 100480
A_TOP8 = 100608
A_WMF = 100624
A_WMB = 100752
A_NM8 = 100816
A_RS8 = 100832
A_TS8 = 100848
A_T8A = 100864
A_I128F = 100992
A_OT2A = 91136
A_OT2B = 95232
A_PT = 99328
A_PBF4 = 100352
A_X6A = 58368
A_X6B = 66560
A_SA = 74752
A_SB = 82944
A_GW = 91136
AL_COLS = 101376


class _S:
    def __init__(self, es, nc, name):
        self.h = es.enter_context(nc.semaphore(name))
        self.n = 0

    def add(self, ins, k=1):
        ins.then_inc(self.h, k)
        self.n += k
        return self.n


def build_program():
    nc = bass.Bass("TRN2", target_bir_lowering=False)

    d_x = nc.dram_tensor("x", [NT + 2, D], F32, kind="ExternalInput").ap()
    d_cw = nc.dram_tensor("cw", [NSUP, NGRP, 8, 128, 256], F32, kind="ExternalInput").ap()
    d_wq = nc.dram_tensor("wq", [1024, 1024], F32, kind="ExternalInput").ap()
    d_wk = nc.dram_tensor("wk", [1024, 1024], F32, kind="ExternalInput").ap()
    d_wv = nc.dram_tensor("wv", [1024, 1024], F32, kind="ExternalInput").ap()
    d_wo = nc.dram_tensor("wo", [1024, 1024], F32, kind="ExternalInput").ap()
    d_pw = nc.dram_tensor("pw", [1024, 4096], F32, kind="ExternalInput").ap()
    d_rw = nc.dram_tensor("rw", [64, 4096], F32, kind="ExternalInput").ap()
    d_eb = nc.dram_tensor("eb", [64, 4096], F32, kind="ExternalInput").ap()
    d_cb = nc.dram_tensor("cbT", [128, 24], F32, kind="ExternalInput").ap()
    d_bq = nc.dram_tensor("bqT", [128, 8], F32, kind="ExternalInput").ap()
    d_bk = nc.dram_tensor("bkT", [128, 8], F32, kind="ExternalInput").ap()
    d_bv = nc.dram_tensor("bvT", [64, 16], F32, kind="ExternalInput").ap()
    d_bo = nc.dram_tensor("boT", [128, 8], F32, kind="ExternalInput").ap()
    d_pb = nc.dram_tensor("pb", [1, 4096], F32, kind="ExternalInput").ap()
    d_rb = nc.dram_tensor("rbT", [64, 1], F32, kind="ExternalInput").ap()
    d_i128 = nc.dram_tensor("i128", [128, 128], F32, kind="ExternalInput").ap()
    d_i64 = nc.dram_tensor("i64", [64, 64], F32, kind="ExternalInput").ap()
    d_mask4 = nc.dram_tensor("mask4", [128, 512], F32, kind="ExternalInput").ap()
    d_ones2 = nc.dram_tensor("ones2", [2, 128], F32, kind="ExternalInput").ap()
    d_out = nc.dram_tensor("out", [NT, D], F32, kind="ExternalOutput").ap()

    with contextlib.ExitStack() as es:
        def sb(name, shape, dt):
            return es.enter_context(nc.sbuf_tensor(name, shape, dt))

        t_i128 = sb("tI128", [128, 128], BF16)
        t_mask4 = sb("tMask4", [128, 512], BF16)
        t_ones2 = sb("tOnes2", [2, 128], BF16)
        t_cbT = sb("tCbT", [128, 24], F32)
        t_bqT = sb("tBqT", [128, 8], F32)
        t_bkT = sb("tBkT", [128, 8], F32)
        t_bvT = sb("tBvT", [64, 16], F32)
        t_boT = sb("tBoT", [128, 8], F32)
        t_rbT = sb("tRbT", [64, 1], F32)
        t_mrv = sb("tMrv", [128, 18], F32)
        t_r = sb("tR", [128, 9], F32)
        t_nmr = sb("tNmr", [128, 9], F32)
        t_wsum = sb("tWsum", [128, 8], F32)
        t_wmT = sb("tWmT", [64, 8 * 128], BF16)
        t_stat = sb("tStat", [128, 9 * 48], F32)
        t_sc1 = sb("tSc1", [128, 1], F32)
        t_sc2 = sb("tSc2", [128, 1], F32)
        t_sc3 = sb("tSc3", [128, 1], F32)
        t_sc4 = sb("tSc4", [128, 1], F32)
        t_eps = sb("tEps", [128, 1], F32)
        t_rs4 = sb("tRs4", [128, 8], F32)
        t_rr4 = sb("tRr4", [128, 8], F32)
        AL = sb("AL", [128, AL_COLS], BF16)

        def bfv(off, cols, parts=128):
            return AL[0:parts, off:off + cols]

        def f32v(off, cols_f32, parts=128):
            return AL[0:parts, off:off + 2 * cols_f32].bitcast(F32)

        xnTh = bfv(A_XNTH, DC * WH)
        xnTl = bfv(A_XNTL, DC * WL)
        qhT = bfv(A_QHT, 16 * 1024, 64)
        khT = bfv(A_KHT, 16 * 1024, 64)
        pwv = bfv(A_PW, 8 * 4096).rearrange("p (c f) -> p c f", c=8)
        cwr = bfv(A_CWR, 4 * 2048).rearrange("p (r f) -> p r f", r=4)
        wb2v = bfv(A_CWR, 8 * 1024).rearrange("p (c f) -> p c f", c=8)
        cvT = bfv(A_CVT, 24 * 1024).rearrange("p (c t) -> p c t", c=24)
        wob = bfv(A_WOB, 16 * 1024, 64).rearrange("p (c f) -> p c f", c=16)
        aoT = bfv(A_AOT, 8 * 1024).rearrange("p (c t) -> p c t", c=8)
        ebv = bfv(A_EB, 4096, 64)
        pbf32 = f32v(A_X6A, 4096, 1)
        pbhi = bfv(A_PBHI, 4096, 1)
        pblo = bfv(A_PBLO, 4096, 1)
        xA = f32v(A_XA, 4096)
        xB = f32v(A_XB, 4096)
        wbv = bfv(A_WB, 8 * 1024).rearrange("p (c f) -> p c f", c=8)
        hiB = bfv(A_HI, 4096)
        loB = bfv(A_LO, 4096)
        vhv = bfv(A_VH, 8 * 1024)
        xnS = f32v(A_XN, 1024)
        rwTh = bfv(A_RWTH, 2048)
        rwTl = bfv(A_RWTL, 2048)
        i64v = f32v(A_I64, 64, 64)
        lgTv = f32v(A_LGT_T, 1024, 64)
        lgt = f32v(A_LGT, 512)
        esc = f32v(A_ESC, 64)
        top8 = f32v(A_TOP8, 8)
        wmf = f32v(A_WMF, 64)
        wmb = bfv(A_WMB, 64)
        nm8 = f32v(A_NM8, 8)
        rs8 = f32v(A_RS8, 8)
        ts8 = f32v(A_TS8, 8)
        t8a = f32v(A_T8A, 64)
        i128f = f32v(A_I128F, 128)
        oT2 = [bfv(A_OT2A, 16 * 256, 64).rearrange("p (c t) -> p c t", c=16),
               bfv(A_OT2B, 16 * 256, 64).rearrange("p (c t) -> p c t", c=16)]
        pTs = bfv(A_PT, 1024)
        pbf = bfv(A_PBF4, 1024)
        x6 = [f32v(A_X6A, 4096), f32v(A_X6B, 4096)]
        s6 = [f32v(A_SA, 4096), f32v(A_SB, 4096)]
        gw = f32v(A_GW, 4096)

        _tapviews = dict(
            xnTh=(xnTh, [128, DC * WH]), xnTl=(xnTl, [128, DC * WL]),
            lgT=(lgTv, [64, 1024]), lgt=(lgt, [128, 512]),
            cvT=(bfv(A_CVT, 24 * 1024), [128, 24 * 1024]),
            qhT=(qhT, [64, 16 * 1024]), khT=(khT, [64, 16 * 1024]),
            vh=(vhv, [128, 8 * 1024]),
            oT2a=(bfv(A_OT2A, 16 * 256, 64), [64, 16 * 256]),
            oT2b=(bfv(A_OT2B, 16 * 256, 64), [64, 16 * 256]),
            aoT=(bfv(A_AOT, 8 * 1024), [128, 8 * 1024]),
        )
        _tapsmall = dict()

        names = ("sCONST sCST2 sXA sXB sTRWh sTRWl sTXh0 sTXh1 sTXl0 sTXl1 "
                 "sTRH sCW0 sCW1 sCW2 sCW3 sWQ sWO sPW sEB sPB32 sX6A sX6B "
                 "sSTa sSTb cRWH cRWL cAG cSQ cMR "
                 "cHI cLO cHAL cPEr cEVr cDVr cRD cPEc cEVc cPE3 cEV3 cS cNM "
                 "cEXP cPS cPT cPTe cOT cOTe cPE5 cEV5 cPE6 cSev cGI cSIL "
                 "cFIN cPB cDVe cWM sWK cPLe cPL cSC3 cXNW").split()
        S = {n: _S(es, nc, n) for n in names}
        (sCONST, sCST2, sXA, sXB, sTRWh, sTRWl, sTXh0, sTXh1, sTXl0, sTXl1,
         sTRH, sCW0, sCW1, sCW2, sCW3, sWQ, sWO, sPW, sEB, sPB32, sX6A, sX6B,
         sSTa, sSTb, cRWH, cRWL, cAG, cSQ,
         cMR, cHI, cLO, cHAL, cPEr, cEVr, cDVr, cRD, cPEc, cEVc, cPE3, cEV3,
         cS, cNM, cEXP, cPS, cPT, cPTe, cOT, cOTe, cPE5, cEV5, cPE6, cSev,
         cGI, cSIL, cFIN, cPB, cDVe, cWM, sWK, cPLe, cPL, cSC3, cXNW) = (
            S[n] for n in names)

        def dedge(e, ins):
            cDVe.add(ins, 1)
            e.wait_ge(cDVe.h, cDVe.n)
        sXsl = [sXA, sXB]
        sTXh = [sTXh0, sTXh1]
        sTXl = [sTXl0, sTXl1]
        sCWs = [sCW0, sCW1, sCW2, sCW3]
        sX6 = [sX6A, sX6B]
        sSTs = [sSTa, sSTb]

        _tapsmall.update(xa=(xA, [128, 4096]), xb=(xB, [128, 4096]),
                         stat=(t_stat[:], [128, 9 * 48]),
                         mrv=(t_mrv[:], [128, 18]), r=(t_r[:], [128, 9]),
                         nmr=(t_nmr[:], [128, 9]), wsum=(t_wsum[:], [128, 8]),
                         wmT=(t_wmT[:], [64, 8 * 128]))
        _tapviews.update(_tapsmall)
        _tapdram = {}
        for tn in TAPS:
            v, shp = _tapviews[tn]
            _tapdram[tn] = nc.dram_tensor("tap_" + tn, shp, F32,
                                          kind="ExternalOutput").ap()

        blk = es.enter_context(nc.Block())
        SY, GP, PE, DV, AC = blk.sync, blk.gpsimd, blk.tensor, blk.vector, blk.scalar

        def _emit_taps():
            if not TAPS:
                return

            def _f(e):
                for s_ in S.values():
                    if s_.n:
                        e.wait_ge(s_.h, s_.n)
                for tn in TAPS:
                    v, shp = _tapviews[tn]
                    sSTa.add(e.dma_start(out=_tapdram[tn], in_=v), 16)
                e.wait_ge(sSTa.h, sSTa.n)
            blk.gpsimd(_f)

        @SY
        def _(e):
            sCONST.add(e.dma_start(out=t_cbT[:], in_=d_cb), 16)
            sCONST.add(e.dma_start(out=t_bqT[:], in_=d_bq), 16)
            sCONST.add(e.dma_start(out=t_bkT[:], in_=d_bk), 16)
            sCONST.add(e.dma_start(out=t_bvT[:], in_=d_bv), 16)
            sCONST.add(e.dma_start(out=t_boT[:], in_=d_bo), 16)
            sCONST.add(e.dma_start(out=t_rbT[:], in_=d_rb), 16)
            sCONST.add(e.dma_start(out=i64v, in_=d_i64), 16)
            sCONST.add(e.dma_start(out=i128f, in_=d_i128), 16)
            sXA.add(e.dma_start(out=xA[0:64, :], in_=d_rw), 16)

        @AC
        def _(e):
            e.wait_ge(sXA.h, 16)
            e.wait_ge(sCONST.h, 128)
            cRWH.add(e.copy(hiB[0:64, :], xA[0:64, :]), 1)

        @DV
        def _(e):
            e.memset(t_eps[:], EPS)
            e.wait_ge(cRWH.h, 1)
            cRWL.add(e.tensor_sub(loB[0:64, :], xA[0:64, :], hiB[0:64, :]), 1)
            cMR.add(e.tensor_scalar(t_bqT[:], t_bqT[:], 0.125, None, ALU.mult), 1)

        @SY
        def _(e):
            e.wait_ge(cRWH.h, 1)
            sTRWh.add(e.dma_start_transpose(
                rwTh.rearrange("p (c f) -> p c f", c=DC), hiB[0:64, :]), 16)
            e.wait_ge(cRWL.h, 1)
            sTRWl.add(e.dma_start_transpose(
                rwTl.rearrange("p (c f) -> p c f", c=DC), loB[0:64, :]), 16)
            sXA.add(e.dma_start(out=xA, in_=d_x[1:129, :]), 16)
            sXB.add(e.dma_start(out=xB, in_=d_x[129:257, :]), 16)

        xhv = xnTh.rearrange("p (c w) -> p c w", w=WH)
        xlv = xnTl.rearrange("p (c w) -> p c w", w=WL)
        xslots = [xA, xB]

        es_r = contextlib.ExitStack()
        p_lgT = es_r.enter_context(nc.psum_tensor("pLgT", [64, 1024], F32))
        p_lgt = es_r.enter_context(nc.psum_tensor("pLgt", [128, 512], F32))
        p_wmT = es_r.enter_context(nc.psum_tensor("pWmT", [64, 1024], F32))

        for j in range(NTILE + 1):
            halo = (j == NTILE)
            xt = xslots[j % 2]
            rows = slice(0, 16) if halo else slice(0, 128)
            jc = 8 if halo else j

            @DV
            def _(e, j=j, xt=xt, rows=rows, jc=jc):
                if j % 2 == 0:
                    e.wait_ge(sXA.h, 16 * (j // 2 + 2))
                else:
                    e.wait_ge(sXB.h, 16 * ((j + 1) // 2))
                sb0 = 48 * jc
                ins = None
                for c in range(8):
                    ins = e.bn_stats(t_stat[rows, sb0 + 6 * c:sb0 + 6 * c + 6],
                                     xt[rows, 512 * c:512 * c + 512])
                dedge(e, ins)
                cAG.add(e.bn_aggr(t_mrv[rows, 2 * jc:2 * jc + 2],
                                  t_stat[rows, sb0:sb0 + 48]), 1)

            @AC
            def _(e, j=j, rows=rows, jc=jc):
                e.wait_ge(cAG.h, j + 1)
                cSQ.add(e.activation(t_sc1[rows, :], t_mrv[rows, 2 * jc + 1:2 * jc + 2],
                                     AF.Sqrt, bias=t_eps[rows, :]), 1)


            @DV
            def _(e, j=j, rows=rows, jc=jc):
                e.wait_ge(cSQ.h, j + 1)
                dedge(e, e.reciprocal(t_r[rows, jc:jc + 1], t_sc1[rows, :]))
                cMR.add(e.tensor_scalar(t_nmr[rows, jc:jc + 1],
                                        t_mrv[rows, 2 * jc:2 * jc + 1],
                                        t_r[rows, jc:jc + 1], -1.0,
                                        ALU.mult, ALU.mult), 1)

            if halo:
                @DV
                def _(e, j=j, xt=xt):
                    e.wait_ge(cMR.h, j + 2)
                    e.wait_ge(sTXh0.h, 16 * NTILE)
                    e.wait_ge(sTXh1.h, 16 * NTILE)
                    cHI.add(e.tensor_scalar(hiB[0:16, :], xt[0:16, :],
                                            t_mrv[0:16, 16:17], t_r[0:16, 8:9],
                                            ALU.subtract, ALU.mult), 1)

                @SY
                def _(e):
                    e.wait_ge(cHI.h, 2 * NTILE + 1)
                    e.wait_ge(sTXl0.h, 16 * NTILE)
                    e.wait_ge(sTXl1.h, 16 * NTILE)
                    sTRH.add(e.dma_start_transpose(
                        loB.rearrange("p (c s) -> p c s", c=DC)[:, :, 0:16],
                        hiB[0:16, :]), 16)

                @DV
                def _(e):
                    e.wait_ge(sTRH.h, 16)
                    hs = loB.rearrange("p (c s) -> p c s", c=DC)
                    e.tensor_copy(xhv[:, :, 15:16], hs[:, :, 0:1])
                    cHAL.add(e.tensor_copy(xhv[:, :, 1040:1041], hs[:, :, 1:2]), 1)
                continue

            for h in range(2):
                @DV
                def _(e, j=j, h=h, xt=xt):
                    e.wait_ge(cMR.h, j + 2)
                    if j == 0:
                        e.wait_ge(sTRWh.h, 16)
                    if j >= 1:
                        e.wait_ge(sTXh[h].h, 16 * j)
                    cHI.add(e.tensor_scalar(hiB[:, 2048 * h:2048 * h + 2048],
                                            xt[:, 2048 * h:2048 * h + 2048],
                                            t_mrv[:, 2 * j:2 * j + 1],
                                            t_r[:, j:j + 1],
                                            ALU.subtract, ALU.mult), 1)

                @DV
                def _(e, j=j, h=h, xt=xt):
                    e.wait_ge(cHI.h, 2 * j + h + 1)
                    if j == 0:
                        e.wait_ge(sTRWl.h, 16)
                    if j >= 1:
                        e.wait_ge(sTXl[h].h, 16 * j)
                    e.wait_ge(cLO.h, 2 * j + h)
                    ins = None
                    for q in range(2):
                        off = 2048 * h + 1024 * q
                        dedge(e, e.tensor_scalar(xnS, xt[:, off:off + 1024],
                                                 t_mrv[:, 2 * j:2 * j + 1],
                                                 t_r[:, j:j + 1],
                                                 ALU.subtract, ALU.mult))
                        ins = e.tensor_sub(loB[:, off:off + 1024], xnS,
                                           hiB[:, off:off + 1024])
                        if q == 0:
                            dedge(e, ins)
                    cLO.add(ins, 1)

            @SY
            def _(e, j=j):
                colh = COLH + 128 * j
                for h in range(2):
                    e.wait_ge(cHI.h, 2 * j + h + 1)
                    sTXh[h].add(e.dma_start_transpose(
                        xhv[:, 16 * h:16 * h + 16, colh:colh + 128],
                        hiB[:, 2048 * h:2048 * h + 2048]), 16)
                for h in range(2):
                    e.wait_ge(cLO.h, 2 * j + h + 1)
                    sTXl[h].add(e.dma_start_transpose(
                        xlv[:, 16 * h:16 * h + 16, 128 * j:128 * j + 128],
                        loB[:, 2048 * h:2048 * h + 2048]), 16)
                jj = j + 2
                if 2 <= jj <= NTILE:
                    if jj < NTILE:
                        r0 = 1 + jj * 128
                        sXsl[jj % 2].add(e.dma_start(out=xslots[jj % 2],
                                                     in_=d_x[r0:r0 + 128, :]), 16)
                    else:
                        sXsl[jj % 2].add(e.dma_start(out=xslots[jj % 2][0:2, :],
                                                     in_=d_x[0::1025, :]), 16)

            if PH >= 2:
                @PE
                def _(e, j=j):
                    for s_ in (sTXh0, sTXh1, sTXl0, sTXl1):
                        e.wait_ge(s_.h, 16 * (j + 1))
                    if j == 0:
                        e.wait_ge(sTRWh.h, 16)
                        e.wait_ge(sTRWl.h, 16)
                    ins = None
                    for p in range(3):
                        lhsT_t = (rwTh, rwTh, rwTl)[p]
                        src_, wid, c0 = ((xnTh, WH, COLH), (xnTl, WL, 0),
                                         (xnTh, WH, COLH))[p]
                        for dcc in range(DC):
                            o = wid * dcc + c0 + 128 * j
                            ins = e.matmul(
                                p_lgT[:, 128 * j:128 * j + 128],
                                lhsT_t[:, 64 * dcc:64 * dcc + 64],
                                src_[:, o:o + 128],
                                start=(j % 4 == 0 and p == 0 and dcc == 0),
                                stop=(p == 2 and dcc == DC - 1),
                                skip_group_check=True)
                    if j == NTILE - 1:
                        cPEr.add(ins, 1)

        if PH < 2:
            _emit_taps()
            return nc
        if True:
            @DV
            def _(e):
                for half in range(2):
                    e.wait_ge(cPEr.h, 1)
                    cEVr.add(e.tensor_scalar(lgTv[:, 512 * half:512 * half + 512],
                                             p_lgT[:, 512 * half:512 * half + 512],
                                             t_rbT[:], None, ALU.add), 1)

            @PE
            def _(e):
                e.wait_ge(cEVr.h, 2)
                ins = None
                for j in range(8):
                    ins = e.matmul(p_lgt[:, 64 * j:64 * j + 64],
                                   lgTv[:, 128 * j:128 * j + 128], i64v,
                                   start=(j == 0), stop=(j == 7))
                cPEr.add(ins, 1)

            @DV
            def _(e):
                e.wait_ge(cPEr.h, 2)
                cEVr.add(e.tensor_copy(lgt, p_lgt[:]), 1)

            lgt3 = lgt.rearrange("p (j f) -> p j f", j=8)
            t8a3 = t8a.rearrange("p (j k) -> p j k", j=8)

            @DV
            def _(e):
                e.wait_ge(cEVr.h, 3)
                dedge(e, e.reduce_max(nm8, lgt3, axis=AX.X, negate=True))
                ins = None
                for j in range(8):
                    ins = e.tensor_scalar(lgt[:, 64 * j:64 * j + 64],
                                          lgt[:, 64 * j:64 * j + 64],
                                          nm8[:, j:j + 1], None, ALU.add)
                cDVr.add(ins, 1)

            @AC
            def _(e):
                e.wait_ge(cDVr.h, 1)
                cEVr.add(e.activation(lgt, lgt, AF.Exp, bias=0.0), 1)

            @DV
            def _(e):
                e.wait_ge(cEVr.h, 4)
                ins = None
                for j in range(8):
                    ins = e.max(t8a3[:, j, :], lgt[:, 64 * j:64 * j + 64])
                dedge(e, ins)
                e.reduce_sum(ts8, t8a3, axis=AX.X)
                dedge(e, e.reduce_sum(rs8, lgt3, axis=AX.X))
                dedge(e, e.reciprocal(rs8, rs8))
                e.tensor_tensor(t_wsum[:], ts8, rs8, ALU.mult)
                ins = None
                for j in range(8):
                    ins = e.scalar_tensor_tensor(lgt[:, 64 * j:64 * j + 64],
                                                 lgt[:, 64 * j:64 * j + 64],
                                                 t8a[:, 8 * j + 7:8 * j + 8],
                                                 lgt[:, 64 * j:64 * j + 64],
                                                 ALU.is_ge, ALU.mult)
                dedge(e, ins)
                ins = None
                for j in range(8):
                    ins = e.tensor_scalar(lgt[:, 64 * j:64 * j + 64],
                                          lgt[:, 64 * j:64 * j + 64],
                                          rs8[:, j:j + 1], None, ALU.mult)
                cDVr.add(ins, 1)

            @PE
            def _(e):
                e.wait_ge(cDVr.h, 2)
                for j in range(8):
                    bank, q = divmod(j, 4)
                    cWM.add(e.matmul(p_wmT[0:64, 512 * bank + 128 * q:
                                           512 * bank + 128 * q + 128],
                                     lgt[:, 64 * j:64 * j + 64], i128f,
                                     start=(q == 0), stop=(q == 3)), 1)

            @DV
            def _(e):
                for bank in range(2):
                    e.wait_ge(cWM.h, 4 * (bank + 1))
                    cEVr.add(e.tensor_copy(t_wmT[0:64, 512 * bank:512 * bank + 512],
                                           p_wmT[0:64, 512 * bank:512 * bank + 512]),
                             1)
                cRD.add(e.nop(), 2)

        es_r.close()

        @GP
        def _(e):
            sCST2.add(e.dma_start(out=t_i128[:], in_=d_i128), 16)
            sCST2.add(e.dma_start(out=t_mask4[:], in_=d_mask4), 16)
            sCST2.add(e.dma_start(out=t_ones2[:], in_=d_ones2), 16)

        if PH < 3:
            _emit_taps()
            return nc

        @GP
        def _(e):
            for gi in range(NSUP * NGRP):
                s, g = divmod(gi, NGRP)
                if gi == 0:
                    e.wait_ge(cPEr.h, 1)
                if gi >= 4:
                    e.wait_ge(cPEc.h, gi - 3)
                sCWs[gi % 4].add(e.dma_start(
                    out=cwr[:, gi % 4, :].rearrange("p (g f) -> p g f", g=8),
                    in_=d_cw[s, g].rearrange("g p f -> p g f")), 16)

        if PH >= 4:
            @GP
            def _(e):
                sWQ.add(e.dma_start(out=wbv,
                                    in_=d_wq.rearrange("(c p) f -> p c f", p=128)), 16)
                e.wait_ge(cPEc.h, 144)
                sWK.add(e.dma_start(out=wb2v,
                                    in_=d_wk.rearrange("(c p) f -> p c f", p=128)), 16)
                e.wait_ge(cPE3.h, 8)
                sWQ.add(e.dma_start(out=wbv,
                                    in_=d_wv.rearrange("(c p) f -> p c f", p=128)), 16)

        if PH >= 5:
            @GP
            def _(e):
                e.wait_ge(cPE3.h, 16)
                sWO.add(e.dma_start(out=wob,
                                    in_=d_wo.rearrange("(c p) f -> p c f", p=64)), 16)
                e.wait_ge(cS.h, 32)
                sPW.add(e.dma_start(out=pwv,
                                    in_=d_pw.rearrange("(c p) f -> p c f", p=128)), 16)

        if PH >= 6:
            @GP
            def _(e):
                e.wait_ge(cEV5.h, 32)
                sEB.add(e.dma_start(out=ebv, in_=d_eb), 16)

        with nc.psum_tensor("pCv", [128, 8 * 512], F32) as p_cv:
            @PE
            def _(e):
                e.wait_ge(sTRH.h, 16)
                e.wait_ge(cHAL.h, 1)
                e.wait_ge(cEVr.h, 6)
                for s_ in range(NSUP):
                    bs = 4 * (s_ % 2)
                    if s_ >= 2:
                        e.wait_ge(cEVc.h, 4 * (s_ - 1))
                    for g in range(NGRP):
                        gi = s_ * NGRP + g
                        e.wait_ge(sCWs[gi % 4].h, 16 * (gi // 4 + 1))
                        ins = None
                        for k in range(8):
                            step = g * 8 + k
                            t, dcc = divmod(step, DC)
                            for sub in range(2):
                                lhsT = cwr[:, gi % 4, 256 * k + 128 * sub:
                                           256 * k + 128 * sub + 128]
                                for hf in range(2):
                                    o = WH * dcc + 15 + 512 * hf + t
                                    ins = e.matmul(
                                        p_cv[:, 512 * (bs + 2 * sub + hf):
                                             512 * (bs + 2 * sub + hf) + 512],
                                        lhsT, xnTh[:, o:o + 512],
                                        start=(step == 0), stop=(step == 95))
                        cPEc.add(ins, 1)

            @AC
            def _(e):
                for s_ in range(NSUP):
                    bs = 4 * (s_ % 2)
                    e.wait_ge(cPEc.h, NGRP * (s_ + 1))
                    for sub in range(2):
                        ch = 2 * s_ + sub
                        for hf in range(2):
                            cEVc.add(e.activation(
                                cvT[:, ch, 512 * hf:512 * hf + 512],
                                p_cv[:, 512 * (bs + 2 * sub + hf):
                                     512 * (bs + 2 * sub + hf) + 512],
                                AF.Identity, bias=t_cbT[:, ch:ch + 1]), 1)

        def ev3(g):
            if g < 0:
                return 0
            return 4 * (g + 1) if g <= 15 else 64 + 2 * (g - 15)

        if PH < 4:
            _emit_taps()
            return nc
        with nc.psum_tensor("pQ3", [128, 4 * 512], F32) as p_q3:
            for m, (outT, biasT, scale) in enumerate(
                    [(qhT, t_bqT, 0.125), (khT, t_bkT, 1.0)]):
                for hp in range(8):
                    gm = 8 * m + hp
                    bs = 2 * (gm % 2)

                    @PE
                    def _(e, m=m, hp=hp, gm=gm, bs=bs):
                        if gm == 0:
                            e.wait_ge(cEVc.h, 48)
                            e.wait_ge(sWQ.h, 16)
                        if m == 1 and hp == 0:
                            e.wait_ge(sWK.h, 16)
                        if gm >= 2:
                            e.wait_ge(cEV3.h, ev3(gm - 2))
                        wsel = wbv if m == 0 else wb2v
                        ins = None
                        for kc in range(8):
                            lhsT = wsel[:, kc, 128 * hp:128 * hp + 128]
                            for hf in range(2):
                                ins = e.matmul(
                                    p_q3[:, 512 * (bs + hf):512 * (bs + hf) + 512],
                                    lhsT, cvT[:, 8 * m + kc, 512 * hf:512 * hf + 512],
                                    start=(kc == 0), stop=(kc == 7))
                        cPE3.add(ins, 1)

                    @AC
                    def _(e, gm=gm, bs=bs, outT=outT, biasT=biasT, scale=scale):
                        e.wait_ge(cPE3.h, gm + 1)
                        for hf in range(2):
                            for hh in range(2):
                                head = 2 * (gm % 8) + hh
                                cEV3.add(e.activation(
                                    outT[0:64, 1024 * head + 512 * hf:
                                         1024 * head + 512 * hf + 512],
                                    p_q3[64 * hh:64 * hh + 64,
                                         512 * (bs + hf):512 * (bs + hf) + 512],
                                    AF.Identity, scale=scale,
                                    bias=biasT[64 * hh:64 * hh + 64,
                                               (gm % 8):(gm % 8) + 1]), 1)

            for tt in range(8):
                gm = 16 + tt
                bs = 2 * (gm % 2)

                @PE
                def _(e, tt=tt, gm=gm, bs=bs):
                    if tt == 0:
                        e.wait_ge(sWQ.h, 32)
                    e.wait_ge(cEV3.h, ev3(gm - 2))
                    ins = None
                    for kc in range(8):
                        lhsT = cvT[:, 16 + kc, 128 * tt:128 * tt + 128]
                        for hf in range(2):
                            ins = e.matmul(
                                p_q3[:, 512 * (bs + hf):512 * (bs + hf) + 512],
                                lhsT, wbv[:, kc, 512 * hf:512 * hf + 512],
                                start=(kc == 0), stop=(kc == 7))
                    cPE3.add(ins, 1)

                @AC
                def _(e, tt=tt, gm=gm, bs=bs):
                    e.wait_ge(cPE3.h, gm + 1)
                    for hf in range(2):
                        cEV3.add(e.activation(
                            vhv[:, 1024 * tt + 512 * hf:1024 * tt + 512 * hf + 512],
                            p_q3[:, 512 * (bs + hf):512 * (bs + hf) + 512],
                            AF.Copy), 1)

        EV3ALL = 80

        if PH < 5:
            _emit_taps()
            return nc
        with nc.psum_tensor("pS", [128, 2 * 512], F32) as p_S, \
             nc.psum_tensor("pPT", [128, 2 * 512], F32) as p_pT, \
             nc.psum_tensor("pOT", [64, 2 * 512], F32) as p_oT, \
             nc.psum_tensor("pWO", [128, 2 * 512], F32) as p_wo_t:
            p_p = p_S

            nmax = [t_sc1, t_sc2]

            def st_quad(e, Q):
                b, qi = divmod(Q, 4)
                sB = 512 * (Q % 2)
                if Q == 0:
                    e.wait_ge(cEV3.h, EV3ALL)
                    e.wait_ge(sCST2.h, 48)
                if Q >= 2:
                    e.wait_ge(cPS.h, 4 * (Q - 1))
                for i in range(4):
                    head = 4 * qi + i
                    off = 1024 * head + 128 * b
                    e.matmul(p_S[:, sB + 128 * i:sB + 128 * i + 128],
                             qhT[:, off:off + 128], khT[:, off:off + 128],
                             start=(i == 0), stop=False)
                cS.add(e.matmul(p_S[:, sB:sB + 512], t_i128[:], t_mask4[:],
                                start=False, stop=True), 1)

            def st_dve1(e, Q):
                e.wait_ge(cS.h, Q + 1)
                if Q >= 2:
                    e.wait_ge(cEXP.h, Q - 1)
                cNM.add(e.reduce_max(nmax[Q % 2][:],
                                     p_S[:, 512 * (Q % 2):512 * (Q % 2) + 512],
                                     axis=AX.X, negate=True), 1)

            def st_exp(e, Q):
                e.wait_ge(cNM.h, Q + 1)
                sB = 512 * (Q % 2)
                cEXP.add(e.activation(p_p[:, sB:sB + 512], p_S[:, sB:sB + 512],
                                      AF.Exp, bias=nmax[Q % 2][:]), 1)

            def st_dve2(e, Q):
                e.wait_ge(cEXP.h, Q + 1)
                sB = 512 * (Q % 2)
                g = 4 * (Q % 2)
                dedge(e, e.reduce_sum(
                    t_rs4[:, g:g + 4],
                    p_p[:, sB:sB + 512].rearrange("p (h t) -> p h t", h=4),
                    axis=AX.X))
                dedge(e, e.reciprocal(t_rr4[:, g:g + 4], t_rs4[:, g:g + 4]))
                if Q >= 2:
                    e.wait_ge(cPT.h, Q - 1)
                for i in range(4):
                    cPS.add(e.tensor_scalar(pbf[:, sB + 128 * i:sB + 128 * i + 128],
                                            p_p[:, sB + 128 * i:sB + 128 * i + 128],
                                            t_rr4[:, g + i:g + i + 1], None,
                                            ALU.mult), 1)

            def st_pt(e, Q):
                e.wait_ge(cPS.h, 4 * (Q + 1))
                if Q >= 2:
                    e.wait_ge(cPTe.h, Q - 1)
                sB = 512 * (Q % 2)
                ins = None
                for i in range(4):
                    ins = e.matmul(p_pT[:, sB + 128 * i:sB + 128 * i + 128],
                                   pbf[:, sB + 128 * i:sB + 128 * i + 128],
                                   t_i128[:], start=(i == 0), stop=(i == 3))
                cPT.add(ins, 1)

            def st_pte(e, Q):
                e.wait_ge(cPT.h, Q + 1)
                if Q >= 2:
                    e.wait_ge(cOT.h, Q - 1)
                sB = 512 * (Q % 2)
                cPTe.add(e.activation(pTs[:, sB:sB + 512], p_pT[:, sB:sB + 512],
                                      AF.Copy), 1)

            def st_ot(e, Q):
                b, qi = divmod(Q, 4)
                e.wait_ge(cPTe.h, Q + 1)
                if Q == 0:
                    e.wait_ge(cEV3.h, EV3ALL)
                if Q >= 2:
                    e.wait_ge(cOTe.h, 4 * (Q - 1))
                sB = 512 * (Q % 2)
                ins = None
                for i in range(4):
                    head = 4 * qi + i
                    ins = e.matmul(p_oT[0:64, sB + 128 * i:sB + 128 * i + 128],
                                   vhv[:, 1024 * b + 64 * head:
                                       1024 * b + 64 * head + 64],
                                   pTs[:, sB + 128 * i:sB + 128 * i + 128],
                                   start=(i == 0), stop=(i == 3))
                cOT.add(ins, 1)

            def st_ote(e, Q):
                b, qi = divmod(Q, 4)
                e.wait_ge(cOT.h, Q + 1)
                if b // 2 >= 2:
                    e.wait_ge(cPE5.h, 8 * (b // 2 - 1))
                sB = 512 * (Q % 2)
                dst = oT2[(b // 2) % 2]
                for i in range(4):
                    head = 4 * qi + i
                    cOTe.add(e.tensor_scalar(
                        dst[0:64, head, 128 * (b % 2):128 * (b % 2) + 128],
                        p_oT[0:64, sB + 128 * i:sB + 128 * i + 128],
                        t_bvT[:, head:head + 1], None, ALU.add), 1)

            for bp in range(4):
                Qs = list(range(8 * bp, 8 * bp + 8))
                for Q in Qs:
                    @PE
                    def _(e, Q=Q, Qs=Qs):
                        st_quad(e, Q)
                        if Q - 1 in Qs:
                            st_pt(e, Q - 1)
                        if Q - 2 in Qs:
                            st_ot(e, Q - 2)

                    @DV
                    def _(e, Q=Q, Qs=Qs):
                        if Q - 1 in Qs:
                            st_dve2(e, Q - 1)
                        st_dve1(e, Q)

                    @AC
                    def _(e, Q=Q, Qs=Qs):
                        if Q - 2 in Qs:
                            st_pte(e, Q - 2)
                        st_exp(e, Q)

                    @DV
                    def _(e, Q=Q, Qs=Qs):
                        if Q - 3 in Qs:
                            st_ote(e, Q - 3)

                Qm = Qs[-1]

                @DV
                def _(e, Qm=Qm):
                    st_dve2(e, Qm)

                @AC
                def _(e, Qm=Qm):
                    st_pte(e, Qm - 1)

                @PE
                def _(e, Qm=Qm):
                    st_pt(e, Qm)
                    st_ot(e, Qm - 1)

                @AC
                def _(e, Qm=Qm):
                    st_pte(e, Qm)

                @PE
                def _(e, Qm=Qm):
                    st_ot(e, Qm)

                @DV
                def _(e, Qm=Qm):
                    st_ote(e, Qm - 2)
                    st_ote(e, Qm - 1)
                    st_ote(e, Qm)

                @PE
                def _(e, bp=bp, Qm=Qm):
                    e.wait_ge(cOTe.h, 32 * (bp + 1))
                    if bp == 0:
                        e.wait_ge(sWO.h, 16)
                    src = oT2[bp % 2]
                    for c in range(8):
                        wB = 512 * (c % 2)
                        if c >= 2:
                            e.wait_ge(cEV5.h, 8 * bp + c - 1)
                        ins = None
                        for hk in range(16):
                            ins = e.matmul(p_wo_t[:, wB:wB + 256],
                                           wob[:, hk, 128 * c:128 * c + 128],
                                           src[0:64, hk, 0:256],
                                           start=(hk == 0), stop=(hk == 15))
                        cPE5.add(ins, 1)

                @AC
                def _(e, bp=bp):
                    for c in range(8):
                        e.wait_ge(cPE5.h, 8 * bp + c + 1)
                        wB = 512 * (c % 2)
                        cEV5.add(e.activation(aoT[:, c, 256 * bp:256 * bp + 256],
                                              p_wo_t[:, wB:wB + 256], AF.Identity,
                                              bias=t_boT[:, c:c + 1]), 1)

        if PH < 6:
            _emit_taps()
            return nc
        with nc.psum_tensor("pPr", [128, 8 * 512], F32) as p_pr:
            @SY
            def _(e):
                e.wait_ge(cEV5.h, 32)
                e.wait_ge(cOT.h, 32)
                sPB32.add(e.dma_start(out=pbf32, in_=d_pb), 16)
                e.wait_ge(cPB.h, 2)
                sX6A.add(e.dma_start(out=x6[0], in_=d_x[1:129, :]), 16)
                sX6B.add(e.dma_start(out=x6[1], in_=d_x[129:257, :]), 16)

            @AC
            def _(e):
                e.wait_ge(sPB32.h, 16)
                e.wait_ge(cEV5.h, 32)
                cPB.add(e.copy(pbhi[0:1, :], pbf32[0:1, :]), 1)

            @DV
            def _(e):
                e.wait_ge(cPB.h, 1)
                cPB.add(e.tensor_sub(pblo[0:1, :], pbf32[0:1, :], pbhi[0:1, :]), 1)

            for tt in range(8):
                @PE
                def _(e, tt=tt):
                    if tt == 0:
                        e.wait_ge(cEV5.h, 32)
                        e.wait_ge(sCST2.h, 48)
                        e.wait_ge(sPW.h, 16)
                        e.wait_ge(cPB.h, 2)
                        e.wait_ge(cPTe.h, 32)
                        e.wait_ge(cPS.h, 128)
                        e.wait_ge(cOTe.h, 128)
                    for seg in range(8):
                        if tt >= 1:
                            e.wait_ge(cGI.h, 8 * (tt - 1) + seg + 1)
                        for kc in range(8):
                            e.matmul(p_pr[:, 512 * seg:512 * seg + 512],
                                     aoT[:, kc, 128 * tt:128 * tt + 128],
                                     pwv[:, kc, 512 * seg:512 * seg + 512],
                                     start=(kc == 0), stop=False)
                        e.matmul(p_pr[:, 512 * seg:512 * seg + 512],
                                 t_ones2[0:1, :],
                                 pbhi[0:1, 512 * seg:512 * seg + 512],
                                 start=False, stop=False)
                        cPE6.add(e.matmul(p_pr[:, 512 * seg:512 * seg + 512],
                                          t_ones2[0:1, :],
                                          pblo[0:1, 512 * seg:512 * seg + 512],
                                          start=False, stop=True), 1)

                @DV
                def _(e, tt=tt):
                    ins = e.tensor_tensor(t_sc3[:], t_r[:, tt:tt + 1],
                                          t_wsum[:, tt:tt + 1], ALU.mult)
                    cSC3.add(ins, 1)
                    if tt >= 2:
                        e.wait_ge(sSTs[tt % 2].h, 16 * (tt // 2))
                    for seg in range(8):
                        e.wait_ge(cPE6.h, 16 * tt + seg + 1)
                        cSev.add(e.tensor_copy(s6[tt % 2][:, 512 * seg:512 * seg + 512],
                                               p_pr[:, 512 * seg:512 * seg + 512]), 1)

                @PE
                def _(e, tt=tt):
                    if tt == 0:
                        e.wait_ge(sEB.h, 16)
                    for seg in range(8):
                        e.wait_ge(cSev.h, 8 * tt + seg + 1)
                        cPE6.add(e.matmul(p_pr[:, 512 * seg:512 * seg + 512],
                                          t_wmT[0:64, 128 * tt:128 * tt + 128],
                                          ebv[0:64, 512 * seg:512 * seg + 512],
                                          start=True, stop=True), 1)

                @DV
                def _(e, tt=tt):
                    e.wait_ge(cSC3.h, tt + 1)
                    if tt >= 1:
                        e.wait_ge(cPL.h, 2 * tt)
                    e.wait_ge(sX6[tt % 2].h, 16 * (tt // 2 + 1))
                    if tt == 0:
                        e.wait_ge(cEV5.h, 32)
                    dedge(e, e.tensor_scalar(gw, x6[tt % 2],
                                             t_mrv[:, 2 * tt:2 * tt + 1],
                                             t_sc3[:], ALU.subtract, ALU.mult))
                    for seg in range(8):
                        e.wait_ge(cPE6.h, 16 * tt + 8 + seg + 1)
                        cGI.add(e.tensor_add(gw[:, 512 * seg:512 * seg + 512],
                                             gw[:, 512 * seg:512 * seg + 512],
                                             p_pr[:, 512 * seg:512 * seg + 512]), 1)
                    if tt >= 1:
                        e.wait_ge(cPL.h, 2 * tt)
                        cFIN.add(e.tensor_add(s6[(tt - 1) % 2], s6[(tt - 1) % 2],
                                              x6[(tt - 1) % 2]), 2)

                @AC
                def _(e, tt=tt):
                    e.wait_ge(cGI.h, 8 * (tt + 1))
                    cSIL.add(e.activation(gw, gw, AF.Silu), 1)

                @GP
                def _(e, tt=tt):
                    e.wait_ge(cSIL.h, tt + 1)
                    cPL.add(e.tensor_mul(s6[tt % 2][:, 0:2048], s6[tt % 2][:, 0:2048],
                                         gw[:, 0:2048]), 1)

                @DV
                def _(e, tt=tt):
                    e.wait_ge(cSIL.h, tt + 1)
                    dedge(e, e.tensor_mul(s6[tt % 2][:, 2048:4096],
                                          s6[tt % 2][:, 2048:4096],
                                          gw[:, 2048:4096]))
                    cPL.add(e.engine_nop(), 1)

                @SY
                def _(e, tt=tt):
                    if tt >= 1:
                        e.wait_ge(cFIN.h, 2 * tt)
                        sSTs[(tt - 1) % 2].add(
                            e.dma_start(out=d_out[128 * (tt - 1):128 * tt, :],
                                        in_=s6[(tt - 1) % 2]), 16)
                        if tt + 1 < 8:
                            r0 = 1 + 128 * (tt + 1)
                            sX6[(tt + 1) % 2].add(
                                e.dma_start(out=x6[(tt + 1) % 2],
                                            in_=d_x[r0:r0 + 128, :]), 16)

            @DV
            def _(e):
                e.wait_ge(cPL.h, 16)
                cFIN.add(e.tensor_add(s6[7 % 2], s6[7 % 2], x6[7 % 2]), 2)

            @SY
            def _(e):
                e.wait_ge(cFIN.h, 16)
                sSTs[7 % 2].add(e.dma_start(out=d_out[128 * 7:128 * 8, :],
                                            in_=s6[7 % 2]), 16)
                e.wait_ge(sSTa.h, 16 * 4)
                e.wait_ge(sSTb.h, 16 * 4)

        _emit_taps()
    return nc



def _prep(inputs):
    f32 = np.float32
    x = np.asarray(inputs["L0_attn_inputs"], f32).reshape(8192, D)
    conv_w = np.asarray(inputs["L0_attn_conv_w"], f32)
    cw = np.ascontiguousarray(
        conv_w.reshape(3, 32, 128, NSUP, 256).transpose(3, 0, 1, 2, 4)
        .reshape(NSUP, NGRP, 8, 128, 256))
    wq = np.ascontiguousarray(np.asarray(inputs["L0_attn_wq"], f32).reshape(1024, 1024))
    wk = np.ascontiguousarray(np.asarray(inputs["L0_attn_wk"], f32).reshape(1024, 1024))
    wv = np.ascontiguousarray(np.asarray(inputs["L0_attn_wv"], f32).reshape(1024, 1024))
    wo = np.ascontiguousarray(np.asarray(inputs["L0_attn_wo"], f32).reshape(1024, 1024))
    pwm = np.ascontiguousarray(np.asarray(inputs["L0_attn_proj_w"], f32))
    rw = np.ascontiguousarray(np.asarray(inputs["L0_ffn_router_weight"], f32))
    eb = np.ascontiguousarray(np.asarray(inputs["L0_ffn_expert_bias"], f32))
    cbT = np.ascontiguousarray(np.asarray(inputs["L0_attn_conv_b"], f32).reshape(24, 128).T)
    bqT = np.ascontiguousarray(np.asarray(inputs["L0_attn_bq"], f32).reshape(8, 128).T)
    bkT = np.ascontiguousarray(np.asarray(inputs["L0_attn_bk"], f32).reshape(8, 128).T)
    bvT = np.ascontiguousarray(np.asarray(inputs["L0_attn_bv"], f32).reshape(16, 64).T)
    boT = np.ascontiguousarray(np.asarray(inputs["L0_attn_bo"], f32).reshape(8, 128).T)
    pb = np.ascontiguousarray(np.asarray(inputs["L0_attn_proj_b"], f32).reshape(1, 4096))
    rbT = np.ascontiguousarray(np.asarray(inputs["L0_ffn_router_bias"], f32).reshape(64, 1))

    i128 = np.eye(128, dtype=f32)
    i64 = np.eye(64, dtype=f32)
    tril = np.tril(np.ones((128, 128), bool))
    m1 = np.where(tril, np.float32(0.0), np.float32(-1e9))
    mask4 = np.ascontiguousarray(np.concatenate([m1] * 4, axis=1))
    ones2 = np.ones((2, 128), dtype=f32)

    common = dict(cw=cw, wq=wq, wk=wk, wv=wv, wo=wo, pw=pwm, rw=rw, eb=eb,
                  cbT=cbT, bqT=bqT, bkT=bkT, bvT=bvT, boT=boT, pb=pb, rbT=rbT,
                  i128=i128, i64=i64, mask4=mask4, ones2=ones2)

    in_maps = []
    zrow = np.zeros((1, D), f32)
    for c in range(8):
        lo = c * 1024
        left = zrow if (lo % 4096) == 0 else x[lo - 1:lo]
        right = zrow if ((lo + 1024) % 4096) == 0 else x[lo + 1024:lo + 1025]
        shard = np.ascontiguousarray(np.concatenate([left, x[lo:lo + 1024], right], 0))
        m = dict(common)
        m["x"] = shard
        in_maps.append(m)
    return in_maps


def kernel(**inputs):
    in_maps = _prep(inputs)
    nc = build_program()
    res = run_bass_kernel_spmd(nc, in_maps, core_ids=list(range(8)))
    out = np.concatenate([r["out"] for r in res.results], axis=0)
    return out.reshape(2, 4096, 4096)


if __name__ == "__main__":
    nc = build_program()
    print("instructions:", nc.n_instructions())


# version 15
# speedup vs baseline: 1.2071x; 1.2071x over previous
import contextlib
import numpy as np
import ml_dtypes
import concourse.bass as bass
import concourse.mybir as mybir
from concourse.bass_utils import run_bass_kernel_spmd

F32 = mybir.dt.float32
BF16 = mybir.dt.bfloat16
AF = mybir.ActivationFunctionType
ALU = mybir.AluOpType
AX = mybir.AxisListType

NT = 1024
NTILE = 8
D = 4096
DC = 32
WH = 1056
WL = 1024
COLH = 16
NSUP = 12
NGRP = 12
EPS = 1e-5
PH = 6
TAPS = []

A_XNTH = 0
A_QHT = 0
A_KHT = 16384
A_PW = 0
A_XNTL = 33792
A_CWR = 33792
A_CVT = 41984
A_WOB = 33792
A_AOT = 50176
A_EB = 32768
A_PBLO = 36864
A_PBHI = 45056
A_XA = 66560
A_XB = 74752
A_WB = 66560
A_HI = 82944
A_LO = 87040
A_VH = 82944
A_XN = 91136
A_RWTH = 93184
A_RWTL = 95232
A_I64 = 97280
A_LGT_T = 97408
A_LGT = 99456
A_ESC = 100480
A_TOP8 = 100608
A_WMF = 100624
A_WMB = 100752
A_NM8 = 100816
A_RS8 = 100832
A_TS8 = 100848
A_T8A = 100864
A_I128F = 100992
A_OT2A = 91136
A_OT2B = 95232
A_PT = 99328
A_PBF4 = 100352
A_X6A = 58368
A_X6B = 66560
A_SA = 74752
A_SB = 82944
A_GW = 91136
AL_COLS = 101376


class _S:
    def __init__(self, es, nc, name):
        self.h = es.enter_context(nc.semaphore(name))
        self.n = 0

    def add(self, ins, k=1):
        ins.then_inc(self.h, k)
        self.n += k
        return self.n


def build_program():
    nc = bass.Bass("TRN2", target_bir_lowering=False)

    d_x = nc.dram_tensor("x", [NT + 2, D], F32, kind="ExternalInput").ap()
    d_cw = nc.dram_tensor("cw", [NSUP, NGRP, 8, 128, 256], F32, kind="ExternalInput").ap()
    d_wq = nc.dram_tensor("wq", [1024, 1024], F32, kind="ExternalInput").ap()
    d_wk = nc.dram_tensor("wk", [1024, 1024], F32, kind="ExternalInput").ap()
    d_wv = nc.dram_tensor("wv", [1024, 1024], F32, kind="ExternalInput").ap()
    d_wo = nc.dram_tensor("wo", [1024, 1024], F32, kind="ExternalInput").ap()
    d_pw = nc.dram_tensor("pw", [1024, 4096], F32, kind="ExternalInput").ap()
    d_rw = nc.dram_tensor("rw", [64, 4096], F32, kind="ExternalInput").ap()
    d_eb = nc.dram_tensor("eb", [64, 4096], F32, kind="ExternalInput").ap()
    d_cb = nc.dram_tensor("cbT", [128, 24], F32, kind="ExternalInput").ap()
    d_bq = nc.dram_tensor("bqT", [128, 8], F32, kind="ExternalInput").ap()
    d_bk = nc.dram_tensor("bkT", [128, 8], F32, kind="ExternalInput").ap()
    d_bv = nc.dram_tensor("bvT", [64, 16], F32, kind="ExternalInput").ap()
    d_bo = nc.dram_tensor("boT", [128, 8], F32, kind="ExternalInput").ap()
    d_pb = nc.dram_tensor("pb", [1, 4096], F32, kind="ExternalInput").ap()
    d_rb = nc.dram_tensor("rbT", [64, 1], F32, kind="ExternalInput").ap()
    d_i128 = nc.dram_tensor("i128", [128, 128], F32, kind="ExternalInput").ap()
    d_i64 = nc.dram_tensor("i64", [64, 64], F32, kind="ExternalInput").ap()
    d_mask4 = nc.dram_tensor("mask4", [128, 512], F32, kind="ExternalInput").ap()
    d_ones2 = nc.dram_tensor("ones2", [2, 128], F32, kind="ExternalInput").ap()
    d_out = nc.dram_tensor("out", [NT, D], F32, kind="ExternalOutput").ap()

    with contextlib.ExitStack() as es:
        def sb(name, shape, dt):
            return es.enter_context(nc.sbuf_tensor(name, shape, dt))

        t_i128 = sb("tI128", [128, 128], BF16)
        t_mask4 = sb("tMask4", [128, 512], BF16)
        t_ones2 = sb("tOnes2", [2, 128], BF16)
        t_cbT = sb("tCbT", [128, 24], F32)
        t_bqT = sb("tBqT", [128, 8], F32)
        t_bkT = sb("tBkT", [128, 8], F32)
        t_bvT = sb("tBvT", [64, 16], F32)
        t_boT = sb("tBoT", [128, 8], F32)
        t_rbT = sb("tRbT", [64, 1], F32)
        t_mrv = sb("tMrv", [128, 18], F32)
        t_r = sb("tR", [128, 9], F32)
        t_nmr = sb("tNmr", [128, 9], F32)
        t_wsum = sb("tWsum", [128, 8], F32)
        t_wmT = sb("tWmT", [64, 8 * 128], BF16)
        t_stat = sb("tStat", [128, 9 * 48], F32)
        t_sc1 = sb("tSc1", [128, 1], F32)
        t_sc2 = sb("tSc2", [128, 1], F32)
        t_sc3 = sb("tSc3", [128, 1], F32)
        t_sc4 = sb("tSc4", [128, 1], F32)
        t_eps = sb("tEps", [128, 1], F32)
        t_rs4 = sb("tRs4", [128, 8], F32)
        t_rr4 = sb("tRr4", [128, 8], F32)
        AL = sb("AL", [128, AL_COLS], BF16)

        def bfv(off, cols, parts=128):
            return AL[0:parts, off:off + cols]

        def f32v(off, cols_f32, parts=128):
            return AL[0:parts, off:off + 2 * cols_f32].bitcast(F32)

        xnTh = bfv(A_XNTH, DC * WH)
        xnTl = bfv(A_XNTL, DC * WL)
        qhT = bfv(A_QHT, 16 * 1024, 64)
        khT = bfv(A_KHT, 16 * 1024, 64)
        pwv = bfv(A_PW, 8 * 4096).rearrange("p (c f) -> p c f", c=8)
        cwr = bfv(A_CWR, 4 * 2048).rearrange("p (r f) -> p r f", r=4)
        wb2v = bfv(A_CWR, 8 * 1024).rearrange("p (c f) -> p c f", c=8)
        cvT = bfv(A_CVT, 24 * 1024).rearrange("p (c t) -> p c t", c=24)
        wob = bfv(A_WOB, 16 * 1024, 64).rearrange("p (c f) -> p c f", c=16)
        aoT = bfv(A_AOT, 8 * 1024).rearrange("p (c t) -> p c t", c=8)
        ebv = bfv(A_EB, 4096, 64)
        pbf32 = f32v(A_X6A, 4096, 1)
        pbhi = bfv(A_PBHI, 4096, 1)
        pblo = bfv(A_PBLO, 4096, 1)
        xA = f32v(A_XA, 4096)
        xB = f32v(A_XB, 4096)
        wbv = bfv(A_WB, 8 * 1024).rearrange("p (c f) -> p c f", c=8)
        hiB = bfv(A_HI, 4096)
        loB = bfv(A_LO, 4096)
        vhv = bfv(A_VH, 8 * 1024)
        xnS = f32v(A_XN, 1024)
        rwTh = bfv(A_RWTH, 2048)
        rwTl = bfv(A_RWTL, 2048)
        i64v = f32v(A_I64, 64, 64)
        lgTv = f32v(A_LGT_T, 1024, 64)
        lgt = f32v(A_LGT, 512)
        esc = f32v(A_ESC, 64)
        top8 = f32v(A_TOP8, 8)
        wmf = f32v(A_WMF, 64)
        wmb = bfv(A_WMB, 64)
        nm8 = f32v(A_NM8, 8)
        rs8 = f32v(A_RS8, 8)
        ts8 = f32v(A_TS8, 8)
        t8a = f32v(A_T8A, 64)
        i128f = f32v(A_I128F, 128)
        oT2 = [bfv(A_OT2A, 16 * 256, 64).rearrange("p (c t) -> p c t", c=16),
               bfv(A_OT2B, 16 * 256, 64).rearrange("p (c t) -> p c t", c=16)]
        pTs = bfv(A_PT, 1024)
        pbf = bfv(A_PBF4, 1024)
        x6 = [f32v(A_X6A, 4096), f32v(A_X6B, 4096)]
        s6 = [f32v(A_SA, 4096), f32v(A_SB, 4096)]
        gw = f32v(A_GW, 4096)

        _tapviews = dict(
            xnTh=(xnTh, [128, DC * WH]), xnTl=(xnTl, [128, DC * WL]),
            lgT=(lgTv, [64, 1024]), lgt=(lgt, [128, 512]),
            cvT=(bfv(A_CVT, 24 * 1024), [128, 24 * 1024]),
            qhT=(qhT, [64, 16 * 1024]), khT=(khT, [64, 16 * 1024]),
            vh=(vhv, [128, 8 * 1024]),
            oT2a=(bfv(A_OT2A, 16 * 256, 64), [64, 16 * 256]),
            oT2b=(bfv(A_OT2B, 16 * 256, 64), [64, 16 * 256]),
            aoT=(bfv(A_AOT, 8 * 1024), [128, 8 * 1024]),
        )
        _tapsmall = dict()

        names = ("sCONST sCST2 sXA sXB sTRWh sTRWl sTXh0 sTXh1 sTXl0 sTXl1 "
                 "sTRH sCW0 sCW1 sCW2 sCW3 sWQ sWO sPW sEB sPB32 sX6A sX6B "
                 "sSTa sSTb cRWH cRWL cAG cSQ cMR "
                 "cHI cLO cHAL cPEr cEVr cDVr cRD cPEc cEVc cPE3 cEV3 cS cNM "
                 "cEXP cPS cPT cPTe cOT cOTe cPE5 cEV5 cPE6 cSev cGI cSIL "
                 "cFIN cPB cDVe cWM sWK cPLe cPL cSC3 cXNW cXNA sXGA sXGB").split()
        S = {n: _S(es, nc, n) for n in names}
        (sCONST, sCST2, sXA, sXB, sTRWh, sTRWl, sTXh0, sTXh1, sTXl0, sTXl1,
         sTRH, sCW0, sCW1, sCW2, sCW3, sWQ, sWO, sPW, sEB, sPB32, sX6A, sX6B,
         sSTa, sSTb, cRWH, cRWL, cAG, cSQ,
         cMR, cHI, cLO, cHAL, cPEr, cEVr, cDVr, cRD, cPEc, cEVc, cPE3, cEV3,
         cS, cNM, cEXP, cPS, cPT, cPTe, cOT, cOTe, cPE5, cEV5, cPE6, cSev,
         cGI, cSIL, cFIN, cPB, cDVe, cWM, sWK, cPLe, cPL, cSC3, cXNW,
         cXNA, sXGA, sXGB) = (S[n] for n in names)

        def dedge(e, ins):
            cDVe.add(ins, 1)
            e.wait_ge(cDVe.h, cDVe.n)
        sXsl = [sXA, sXB]
        sXG = [sXGA, sXGB]

        def wait_xload(e, j):
            if j == 0:
                e.wait_ge(sXA.h, 32)
            elif j == 1:
                e.wait_ge(sXB.h, 16)
            elif j % 2 == 0:
                e.wait_ge(sXGA.h, 16 * (j // 2))
            else:
                e.wait_ge(sXGB.h, 16 * ((j - 1) // 2))
        sTXh = [sTXh0, sTXh1]
        sTXl = [sTXl0, sTXl1]
        sCWs = [sCW0, sCW1, sCW2, sCW3]
        sX6 = [sX6A, sX6B]
        sSTs = [sSTa, sSTb]

        _tapsmall.update(xa=(xA, [128, 4096]), xb=(xB, [128, 4096]),
                         stat=(t_stat[:], [128, 9 * 48]),
                         mrv=(t_mrv[:], [128, 18]), r=(t_r[:], [128, 9]),
                         nmr=(t_nmr[:], [128, 9]), wsum=(t_wsum[:], [128, 8]),
                         wmT=(t_wmT[:], [64, 8 * 128]))
        _tapviews.update(_tapsmall)
        _tapdram = {}
        for tn in TAPS:
            v, shp = _tapviews[tn]
            _tapdram[tn] = nc.dram_tensor("tap_" + tn, shp, F32,
                                          kind="ExternalOutput").ap()

        blk = es.enter_context(nc.Block())
        SY, GP, PE, DV, AC = blk.sync, blk.gpsimd, blk.tensor, blk.vector, blk.scalar

        def _emit_taps():
            if not TAPS:
                return

            def _f(e):
                for s_ in S.values():
                    if s_.n:
                        e.wait_ge(s_.h, s_.n)
                for tn in TAPS:
                    v, shp = _tapviews[tn]
                    sSTa.add(e.dma_start(out=_tapdram[tn], in_=v), 16)
                e.wait_ge(sSTa.h, sSTa.n)
            blk.gpsimd(_f)

        @SY
        def _(e):
            sCONST.add(e.dma_start(out=t_cbT[:], in_=d_cb), 16)
            sCONST.add(e.dma_start(out=t_bqT[:], in_=d_bq), 16)
            sCONST.add(e.dma_start(out=t_bkT[:], in_=d_bk), 16)
            sCONST.add(e.dma_start(out=t_bvT[:], in_=d_bv), 16)
            sCONST.add(e.dma_start(out=t_boT[:], in_=d_bo), 16)
            sCONST.add(e.dma_start(out=t_rbT[:], in_=d_rb), 16)
            sCONST.add(e.dma_start(out=i64v, in_=d_i64), 16)
            sCONST.add(e.dma_start(out=i128f, in_=d_i128), 16)
            sXA.add(e.dma_start(out=xA[0:64, :], in_=d_rw), 16)

        @AC
        def _(e):
            e.wait_ge(sXA.h, 16)
            e.wait_ge(sCONST.h, 128)
            cRWH.add(e.copy(hiB[0:64, :], xA[0:64, :]), 1)

        @DV
        def _(e):
            e.memset(t_eps[:], EPS)
            e.wait_ge(cRWH.h, 1)
            cRWL.add(e.tensor_sub(loB[0:64, :], xA[0:64, :], hiB[0:64, :]), 1)
            cMR.add(e.tensor_scalar(t_bqT[:], t_bqT[:], 0.125, None, ALU.mult), 1)

        @SY
        def _(e):
            e.wait_ge(cRWH.h, 1)
            sTRWh.add(e.dma_start_transpose(
                rwTh.rearrange("p (c f) -> p c f", c=DC), hiB[0:64, :]), 16)
            e.wait_ge(cRWL.h, 1)
            sTRWl.add(e.dma_start_transpose(
                rwTl.rearrange("p (c f) -> p c f", c=DC), loB[0:64, :]), 16)
            sXA.add(e.dma_start(out=xA, in_=d_x[1:129, :]), 16)
            sXB.add(e.dma_start(out=xB, in_=d_x[129:257, :]), 16)

        xhv = xnTh.rearrange("p (c w) -> p c w", w=WH)
        xlv = xnTl.rearrange("p (c w) -> p c w", w=WL)
        xslots = [xA, xB]
        xnSp = [f32v(A_XN, 1024), f32v(A_LGT_T, 1024)]

        es_r = contextlib.ExitStack()
        p_lgT = es_r.enter_context(nc.psum_tensor("pLgT", [64, 1024], F32))
        p_lgt = es_r.enter_context(nc.psum_tensor("pLgt", [128, 512], F32))
        p_wmT = es_r.enter_context(nc.psum_tensor("pWmT", [64, 1024], F32))

        for j in range(NTILE + 1):
            halo = (j == NTILE)
            xt = xslots[j % 2]
            rows = slice(0, 16) if halo else slice(0, 128)
            jc = 8 if halo else j

            @DV
            def _(e, j=j, xt=xt, rows=rows, jc=jc):
                wait_xload(e, j)
                sb0 = 48 * jc
                ins = None
                for c in range(8):
                    ins = e.bn_stats(t_stat[rows, sb0 + 6 * c:sb0 + 6 * c + 6],
                                     xt[rows, 512 * c:512 * c + 512])
                dedge(e, ins)
                cAG.add(e.bn_aggr(t_mrv[rows, 2 * jc:2 * jc + 2],
                                  t_stat[rows, sb0:sb0 + 48]), 1)

            @AC
            def _(e, j=j, rows=rows, jc=jc):
                e.wait_ge(cAG.h, j + 1)
                cSQ.add(e.activation(t_sc1[rows, :], t_mrv[rows, 2 * jc + 1:2 * jc + 2],
                                     AF.Sqrt, bias=t_eps[rows, :]), 1)


            @DV
            def _(e, j=j, rows=rows, jc=jc):
                e.wait_ge(cSQ.h, j + 1)
                dedge(e, e.reciprocal(t_r[rows, jc:jc + 1], t_sc1[rows, :]))
                cMR.add(e.tensor_scalar(t_nmr[rows, jc:jc + 1],
                                        t_mrv[rows, 2 * jc:2 * jc + 1],
                                        t_r[rows, jc:jc + 1], -1.0,
                                        ALU.mult, ALU.mult), 1)

            if halo:
                @DV
                def _(e, j=j, xt=xt):
                    e.wait_ge(cMR.h, j + 2)
                    e.wait_ge(sTXh0.h, 16 * NTILE)
                    e.wait_ge(sTXh1.h, 16 * NTILE)
                    cHI.add(e.tensor_scalar(hiB[0:16, :], xt[0:16, :],
                                            t_mrv[0:16, 16:17], t_r[0:16, 8:9],
                                            ALU.subtract, ALU.mult), 1)

                @SY
                def _(e):
                    e.wait_ge(cHI.h, 2 * NTILE + 1)
                    e.wait_ge(sTXl0.h, 16 * NTILE)
                    e.wait_ge(sTXl1.h, 16 * NTILE)
                    sTRH.add(e.dma_start_transpose(
                        loB.rearrange("p (c s) -> p c s", c=DC)[:, :, 0:16],
                        hiB[0:16, :]), 16)

                @DV
                def _(e):
                    e.wait_ge(sTRH.h, 16)
                    hs = loB.rearrange("p (c s) -> p c s", c=DC)
                    e.tensor_copy(xhv[:, :, 15:16], hs[:, :, 0:1])
                    cHAL.add(e.tensor_copy(xhv[:, :, 1040:1041], hs[:, :, 1:2]), 1)
                continue

            for h in range(2):
                @DV
                def _(e, j=j, h=h, xt=xt):
                    e.wait_ge(cMR.h, j + 2)
                    if j == 0:
                        e.wait_ge(sTRWh.h, 16)
                    if j >= 1:
                        e.wait_ge(sTXh[h].h, 16 * j)
                    cHI.add(e.tensor_scalar(hiB[:, 2048 * h:2048 * h + 2048],
                                            xt[:, 2048 * h:2048 * h + 2048],
                                            t_mrv[:, 2 * j:2 * j + 1],
                                            t_r[:, j:j + 1],
                                            ALU.subtract, ALU.mult), 1)

                @DV
                def _(e, j=j, h=h, xt=xt):
                    e.wait_ge(cHI.h, 2 * j + h + 1)
                    if j == 0:
                        e.wait_ge(sTRWl.h, 16)
                    if j >= 1:
                        e.wait_ge(sTXl[h].h, 16 * j)
                    ins = None
                    for q in range(2):
                        g = 4 * j + 2 * h + q
                        off = 2048 * h + 1024 * q
                        e.wait_ge(cXNA.h, g + 1)
                        ins = e.tensor_sub(loB[:, off:off + 1024], xnSp[g % 2],
                                           hiB[:, off:off + 1024])
                        cXNW.add(ins, 1)

            @AC
            def _(e, j=j, xt=xt):
                e.wait_ge(cMR.h, j + 2)
                wait_xload(e, j)
                for k in range(4):
                    g = 4 * j + k
                    if g >= 2:
                        e.wait_ge(cXNW.h, g - 1)
                    cXNA.add(e.activation(xnSp[g % 2], xt[:, 1024 * k:1024 * k + 1024],
                                          AF.Identity, scale=t_r[:, j:j + 1],
                                          bias=t_nmr[:, j:j + 1]), 1)

            @SY
            def _(e, j=j):
                colh = COLH + 128 * j
                for h in range(2):
                    e.wait_ge(cHI.h, 2 * j + h + 1)
                    sTXh[h].add(e.dma_start_transpose(
                        xhv[:, 16 * h:16 * h + 16, colh:colh + 128],
                        hiB[:, 2048 * h:2048 * h + 2048]), 16)
                for h in range(2):
                    e.wait_ge(cXNW.h, 4 * j + 2 * h + 2)
                    sTXl[h].add(e.dma_start_transpose(
                        xlv[:, 16 * h:16 * h + 16, 128 * j:128 * j + 128],
                        loB[:, 2048 * h:2048 * h + 2048]), 16)

            @SY
            def _(e, j=j):
                jj = j + 2
                if 2 <= jj <= NTILE:
                    e.wait_ge(cHI.h, 2 * (jj - 2) + 2)
                    e.wait_ge(cXNA.h, 4 * (jj - 2) + 4)
                    if jj < NTILE:
                        r0 = 1 + jj * 128
                        sXG[jj % 2].add(e.dma_start(out=xslots[jj % 2],
                                                    in_=d_x[r0:r0 + 128, :]), 16)
                    else:
                        sXG[jj % 2].add(e.dma_start(out=xslots[jj % 2][0:2, :],
                                                    in_=d_x[0::1025, :]), 16)

            if PH >= 2:
                @PE
                def _(e, j=j):
                    for s_ in (sTXh0, sTXh1, sTXl0, sTXl1):
                        e.wait_ge(s_.h, 16 * (j + 1))
                    if j == 0:
                        e.wait_ge(sTRWh.h, 16)
                        e.wait_ge(sTRWl.h, 16)
                    ins = None
                    for p in range(3):
                        lhsT_t = (rwTh, rwTh, rwTl)[p]
                        src_, wid, c0 = ((xnTh, WH, COLH), (xnTl, WL, 0),
                                         (xnTh, WH, COLH))[p]
                        for dcc in range(DC):
                            o = wid * dcc + c0 + 128 * j
                            ins = e.matmul(
                                p_lgT[:, 128 * j:128 * j + 128],
                                lhsT_t[:, 64 * dcc:64 * dcc + 64],
                                src_[:, o:o + 128],
                                start=(j % 4 == 0 and p == 0 and dcc == 0),
                                stop=(p == 2 and dcc == DC - 1),
                                skip_group_check=True)
                    if j == NTILE - 1:
                        cPEr.add(ins, 1)

        if PH < 2:
            _emit_taps()
            return nc
        if True:
            @DV
            def _(e):
                for half in range(2):
                    e.wait_ge(cPEr.h, 1)
                    cEVr.add(e.tensor_scalar(lgTv[:, 512 * half:512 * half + 512],
                                             p_lgT[:, 512 * half:512 * half + 512],
                                             t_rbT[:], None, ALU.add), 1)

            @PE
            def _(e):
                e.wait_ge(cEVr.h, 2)
                ins = None
                for j in range(8):
                    ins = e.matmul(p_lgt[:, 64 * j:64 * j + 64],
                                   lgTv[:, 128 * j:128 * j + 128], i64v,
                                   start=(j == 0), stop=(j == 7))
                cPEr.add(ins, 1)

            @DV
            def _(e):
                e.wait_ge(cPEr.h, 2)
                cEVr.add(e.tensor_copy(lgt, p_lgt[:]), 1)

            lgt3 = lgt.rearrange("p (j f) -> p j f", j=8)
            t8a3 = t8a.rearrange("p (j k) -> p j k", j=8)

            @DV
            def _(e):
                e.wait_ge(cEVr.h, 3)
                dedge(e, e.reduce_max(nm8, lgt3, axis=AX.X, negate=True))
                ins = None
                for j in range(8):
                    ins = e.tensor_scalar(lgt[:, 64 * j:64 * j + 64],
                                          lgt[:, 64 * j:64 * j + 64],
                                          nm8[:, j:j + 1], None, ALU.add)
                cDVr.add(ins, 1)

            @AC
            def _(e):
                e.wait_ge(cDVr.h, 1)
                cEVr.add(e.activation(lgt, lgt, AF.Exp, bias=0.0), 1)

            @DV
            def _(e):
                e.wait_ge(cEVr.h, 4)
                ins = None
                for j in range(8):
                    ins = e.max(t8a3[:, j, :], lgt[:, 64 * j:64 * j + 64])
                dedge(e, ins)
                e.reduce_sum(ts8, t8a3, axis=AX.X)
                dedge(e, e.reduce_sum(rs8, lgt3, axis=AX.X))
                dedge(e, e.reciprocal(rs8, rs8))
                e.tensor_tensor(t_wsum[:], ts8, rs8, ALU.mult)
                ins = None
                for j in range(8):
                    ins = e.scalar_tensor_tensor(lgt[:, 64 * j:64 * j + 64],
                                                 lgt[:, 64 * j:64 * j + 64],
                                                 t8a[:, 8 * j + 7:8 * j + 8],
                                                 lgt[:, 64 * j:64 * j + 64],
                                                 ALU.is_ge, ALU.mult)
                dedge(e, ins)
                ins = None
                for j in range(8):
                    ins = e.tensor_scalar(lgt[:, 64 * j:64 * j + 64],
                                          lgt[:, 64 * j:64 * j + 64],
                                          rs8[:, j:j + 1], None, ALU.mult)
                cDVr.add(ins, 1)

            @PE
            def _(e):
                e.wait_ge(cDVr.h, 2)
                for j in range(8):
                    bank, q = divmod(j, 4)
                    cWM.add(e.matmul(p_wmT[0:64, 512 * bank + 128 * q:
                                           512 * bank + 128 * q + 128],
                                     lgt[:, 64 * j:64 * j + 64], i128f,
                                     start=(q == 0), stop=(q == 3)), 1)

            @DV
            def _(e):
                for bank in range(2):
                    e.wait_ge(cWM.h, 4 * (bank + 1))
                    cEVr.add(e.tensor_copy(t_wmT[0:64, 512 * bank:512 * bank + 512],
                                           p_wmT[0:64, 512 * bank:512 * bank + 512]),
                             1)
                cRD.add(e.nop(), 2)

        es_r.close()

        @GP
        def _(e):
            sCST2.add(e.dma_start(out=t_i128[:], in_=d_i128), 16)
            sCST2.add(e.dma_start(out=t_mask4[:], in_=d_mask4), 16)
            sCST2.add(e.dma_start(out=t_ones2[:], in_=d_ones2), 16)

        if PH < 3:
            _emit_taps()
            return nc

        @GP
        def _(e):
            for gi in range(NSUP * NGRP):
                s, g = divmod(gi, NGRP)
                if gi == 0:
                    e.wait_ge(cPEr.h, 1)
                if gi >= 4:
                    e.wait_ge(cPEc.h, gi - 3)
                sCWs[gi % 4].add(e.dma_start(
                    out=cwr[:, gi % 4, :].rearrange("p (g f) -> p g f", g=8),
                    in_=d_cw[s, g].rearrange("g p f -> p g f")), 16)

        if PH >= 4:
            @GP
            def _(e):
                sWQ.add(e.dma_start(out=wbv,
                                    in_=d_wq.rearrange("(c p) f -> p c f", p=128)), 16)
                e.wait_ge(cPEc.h, 144)
                sWK.add(e.dma_start(out=wb2v,
                                    in_=d_wk.rearrange("(c p) f -> p c f", p=128)), 16)
                e.wait_ge(cPE3.h, 8)
                sWQ.add(e.dma_start(out=wbv,
                                    in_=d_wv.rearrange("(c p) f -> p c f", p=128)), 16)

        if PH >= 5:
            @GP
            def _(e):
                e.wait_ge(cPE3.h, 16)
                sWO.add(e.dma_start(out=wob,
                                    in_=d_wo.rearrange("(c p) f -> p c f", p=64)), 16)
                e.wait_ge(cS.h, 32)
                sPW.add(e.dma_start(out=pwv,
                                    in_=d_pw.rearrange("(c p) f -> p c f", p=128)), 16)

        if PH >= 6:
            @GP
            def _(e):
                e.wait_ge(cEV5.h, 32)
                sEB.add(e.dma_start(out=ebv, in_=d_eb), 16)

        with nc.psum_tensor("pCv", [128, 8 * 512], F32) as p_cv:
            @PE
            def _(e):
                e.wait_ge(sTRH.h, 16)
                e.wait_ge(cHAL.h, 1)
                e.wait_ge(cEVr.h, 6)
                for s_ in range(NSUP):
                    bs = 4 * (s_ % 2)
                    if s_ >= 2:
                        e.wait_ge(cEVc.h, 4 * (s_ - 1))
                    for g in range(NGRP):
                        gi = s_ * NGRP + g
                        e.wait_ge(sCWs[gi % 4].h, 16 * (gi // 4 + 1))
                        ins = None
                        for k in range(8):
                            step = g * 8 + k
                            t, dcc = divmod(step, DC)
                            for sub in range(2):
                                lhsT = cwr[:, gi % 4, 256 * k + 128 * sub:
                                           256 * k + 128 * sub + 128]
                                for hf in range(2):
                                    o = WH * dcc + 15 + 512 * hf + t
                                    ins = e.matmul(
                                        p_cv[:, 512 * (bs + 2 * sub + hf):
                                             512 * (bs + 2 * sub + hf) + 512],
                                        lhsT, xnTh[:, o:o + 512],
                                        start=(step == 0), stop=(step == 95))
                        cPEc.add(ins, 1)

            @AC
            def _(e):
                for s_ in range(NSUP):
                    bs = 4 * (s_ % 2)
                    e.wait_ge(cPEc.h, NGRP * (s_ + 1))
                    for sub in range(2):
                        ch = 2 * s_ + sub
                        for hf in range(2):
                            cEVc.add(e.activation(
                                cvT[:, ch, 512 * hf:512 * hf + 512],
                                p_cv[:, 512 * (bs + 2 * sub + hf):
                                     512 * (bs + 2 * sub + hf) + 512],
                                AF.Identity, bias=t_cbT[:, ch:ch + 1]), 1)

        def ev3(g):
            if g < 0:
                return 0
            return 4 * (g + 1) if g <= 15 else 64 + 2 * (g - 15)

        if PH < 4:
            _emit_taps()
            return nc
        with nc.psum_tensor("pQ3", [128, 4 * 512], F32) as p_q3:
            for m, (outT, biasT, scale) in enumerate(
                    [(qhT, t_bqT, 0.125), (khT, t_bkT, 1.0)]):
                for hp in range(8):
                    gm = 8 * m + hp
                    bs = 2 * (gm % 2)

                    @PE
                    def _(e, m=m, hp=hp, gm=gm, bs=bs):
                        if gm == 0:
                            e.wait_ge(cEVc.h, 48)
                            e.wait_ge(sWQ.h, 16)
                        if m == 1 and hp == 0:
                            e.wait_ge(sWK.h, 16)
                        if gm >= 2:
                            e.wait_ge(cEV3.h, ev3(gm - 2))
                        wsel = wbv if m == 0 else wb2v
                        ins = None
                        for kc in range(8):
                            lhsT = wsel[:, kc, 128 * hp:128 * hp + 128]
                            for hf in range(2):
                                ins = e.matmul(
                                    p_q3[:, 512 * (bs + hf):512 * (bs + hf) + 512],
                                    lhsT, cvT[:, 8 * m + kc, 512 * hf:512 * hf + 512],
                                    start=(kc == 0), stop=(kc == 7))
                        cPE3.add(ins, 1)

                    @AC
                    def _(e, gm=gm, bs=bs, outT=outT, biasT=biasT, scale=scale):
                        e.wait_ge(cPE3.h, gm + 1)
                        for hf in range(2):
                            for hh in range(2):
                                head = 2 * (gm % 8) + hh
                                cEV3.add(e.activation(
                                    outT[0:64, 1024 * head + 512 * hf:
                                         1024 * head + 512 * hf + 512],
                                    p_q3[64 * hh:64 * hh + 64,
                                         512 * (bs + hf):512 * (bs + hf) + 512],
                                    AF.Identity, scale=scale,
                                    bias=biasT[64 * hh:64 * hh + 64,
                                               (gm % 8):(gm % 8) + 1]), 1)

            for tt in range(8):
                gm = 16 + tt
                bs = 2 * (gm % 2)

                @PE
                def _(e, tt=tt, gm=gm, bs=bs):
                    if tt == 0:
                        e.wait_ge(sWQ.h, 32)
                    e.wait_ge(cEV3.h, ev3(gm - 2))
                    ins = None
                    for kc in range(8):
                        lhsT = cvT[:, 16 + kc, 128 * tt:128 * tt + 128]
                        for hf in range(2):
                            ins = e.matmul(
                                p_q3[:, 512 * (bs + hf):512 * (bs + hf) + 512],
                                lhsT, wbv[:, kc, 512 * hf:512 * hf + 512],
                                start=(kc == 0), stop=(kc == 7))
                    cPE3.add(ins, 1)

                @AC
                def _(e, tt=tt, gm=gm, bs=bs):
                    e.wait_ge(cPE3.h, gm + 1)
                    for hf in range(2):
                        cEV3.add(e.activation(
                            vhv[:, 1024 * tt + 512 * hf:1024 * tt + 512 * hf + 512],
                            p_q3[:, 512 * (bs + hf):512 * (bs + hf) + 512],
                            AF.Copy), 1)

        EV3ALL = 80

        if PH < 5:
            _emit_taps()
            return nc
        with nc.psum_tensor("pS", [128, 2 * 512], F32) as p_S, \
             nc.psum_tensor("pPT", [128, 2 * 512], F32) as p_pT, \
             nc.psum_tensor("pOT", [64, 2 * 512], F32) as p_oT, \
             nc.psum_tensor("pWO", [128, 2 * 512], F32) as p_wo_t:
            p_p = p_S

            nmax = [t_sc1, t_sc2]

            def st_quad(e, Q):
                b, qi = divmod(Q, 4)
                sB = 512 * (Q % 2)
                if Q == 0:
                    e.wait_ge(cEV3.h, EV3ALL)
                    e.wait_ge(sCST2.h, 48)
                if Q >= 2:
                    e.wait_ge(cPS.h, 4 * (Q - 1))
                for i in range(4):
                    head = 4 * qi + i
                    off = 1024 * head + 128 * b
                    e.matmul(p_S[:, sB + 128 * i:sB + 128 * i + 128],
                             qhT[:, off:off + 128], khT[:, off:off + 128],
                             start=(i == 0), stop=False)
                cS.add(e.matmul(p_S[:, sB:sB + 512], t_i128[:], t_mask4[:],
                                start=False, stop=True), 1)

            def st_dve1(e, Q):
                e.wait_ge(cS.h, Q + 1)
                if Q >= 2:
                    e.wait_ge(cEXP.h, Q - 1)
                cNM.add(e.reduce_max(nmax[Q % 2][:],
                                     p_S[:, 512 * (Q % 2):512 * (Q % 2) + 512],
                                     axis=AX.X, negate=True), 1)

            def st_exp(e, Q):
                e.wait_ge(cNM.h, Q + 1)
                sB = 512 * (Q % 2)
                cEXP.add(e.activation(p_p[:, sB:sB + 512], p_S[:, sB:sB + 512],
                                      AF.Exp, bias=nmax[Q % 2][:]), 1)

            def st_dve2(e, Q):
                e.wait_ge(cEXP.h, Q + 1)
                sB = 512 * (Q % 2)
                g = 4 * (Q % 2)
                dedge(e, e.reduce_sum(
                    t_rs4[:, g:g + 4],
                    p_p[:, sB:sB + 512].rearrange("p (h t) -> p h t", h=4),
                    axis=AX.X))
                dedge(e, e.reciprocal(t_rr4[:, g:g + 4], t_rs4[:, g:g + 4]))
                if Q >= 2:
                    e.wait_ge(cPT.h, Q - 1)
                for i in range(4):
                    cPS.add(e.tensor_scalar(pbf[:, sB + 128 * i:sB + 128 * i + 128],
                                            p_p[:, sB + 128 * i:sB + 128 * i + 128],
                                            t_rr4[:, g + i:g + i + 1], None,
                                            ALU.mult), 1)

            def st_pt(e, Q):
                e.wait_ge(cPS.h, 4 * (Q + 1))
                if Q >= 2:
                    e.wait_ge(cPTe.h, Q - 1)
                sB = 512 * (Q % 2)
                ins = None
                for i in range(4):
                    ins = e.matmul(p_pT[:, sB + 128 * i:sB + 128 * i + 128],
                                   pbf[:, sB + 128 * i:sB + 128 * i + 128],
                                   t_i128[:], start=(i == 0), stop=(i == 3))
                cPT.add(ins, 1)

            def st_pte(e, Q):
                e.wait_ge(cPT.h, Q + 1)
                if Q >= 2:
                    e.wait_ge(cOT.h, Q - 1)
                sB = 512 * (Q % 2)
                cPTe.add(e.activation(pTs[:, sB:sB + 512], p_pT[:, sB:sB + 512],
                                      AF.Copy), 1)

            def st_ot(e, Q):
                b, qi = divmod(Q, 4)
                e.wait_ge(cPTe.h, Q + 1)
                if Q == 0:
                    e.wait_ge(cEV3.h, EV3ALL)
                if Q >= 2:
                    e.wait_ge(cOTe.h, 4 * (Q - 1))
                sB = 512 * (Q % 2)
                ins = None
                for i in range(4):
                    head = 4 * qi + i
                    ins = e.matmul(p_oT[0:64, sB + 128 * i:sB + 128 * i + 128],
                                   vhv[:, 1024 * b + 64 * head:
                                       1024 * b + 64 * head + 64],
                                   pTs[:, sB + 128 * i:sB + 128 * i + 128],
                                   start=(i == 0), stop=(i == 3))
                cOT.add(ins, 1)

            def st_ote(e, Q):
                b, qi = divmod(Q, 4)
                e.wait_ge(cOT.h, Q + 1)
                if b // 2 >= 2:
                    e.wait_ge(cPE5.h, 8 * (b // 2 - 1))
                sB = 512 * (Q % 2)
                dst = oT2[(b // 2) % 2]
                for i in range(4):
                    head = 4 * qi + i
                    cOTe.add(e.tensor_scalar(
                        dst[0:64, head, 128 * (b % 2):128 * (b % 2) + 128],
                        p_oT[0:64, sB + 128 * i:sB + 128 * i + 128],
                        t_bvT[:, head:head + 1], None, ALU.add), 1)

            for bp in range(4):
                Qs = list(range(8 * bp, 8 * bp + 8))
                for Q in Qs:
                    @PE
                    def _(e, Q=Q, Qs=Qs):
                        st_quad(e, Q)
                        if Q - 1 in Qs:
                            st_pt(e, Q - 1)
                        if Q - 2 in Qs:
                            st_ot(e, Q - 2)

                    @DV
                    def _(e, Q=Q, Qs=Qs):
                        if Q - 1 in Qs:
                            st_dve2(e, Q - 1)
                        st_dve1(e, Q)

                    @AC
                    def _(e, Q=Q, Qs=Qs):
                        if Q - 2 in Qs:
                            st_pte(e, Q - 2)
                        st_exp(e, Q)

                    @DV
                    def _(e, Q=Q, Qs=Qs):
                        if Q - 3 in Qs:
                            st_ote(e, Q - 3)

                Qm = Qs[-1]

                @DV
                def _(e, Qm=Qm):
                    st_dve2(e, Qm)

                @AC
                def _(e, Qm=Qm):
                    st_pte(e, Qm - 1)

                @PE
                def _(e, Qm=Qm):
                    st_pt(e, Qm)
                    st_ot(e, Qm - 1)

                @AC
                def _(e, Qm=Qm):
                    st_pte(e, Qm)

                @PE
                def _(e, Qm=Qm):
                    st_ot(e, Qm)

                @DV
                def _(e, Qm=Qm):
                    st_ote(e, Qm - 2)
                    st_ote(e, Qm - 1)
                    st_ote(e, Qm)

                @PE
                def _(e, bp=bp, Qm=Qm):
                    e.wait_ge(cOTe.h, 32 * (bp + 1))
                    if bp == 0:
                        e.wait_ge(sWO.h, 16)
                    src = oT2[bp % 2]
                    for c in range(8):
                        wB = 512 * (c % 2)
                        if c >= 2:
                            e.wait_ge(cEV5.h, 8 * bp + c - 1)
                        ins = None
                        for hk in range(16):
                            ins = e.matmul(p_wo_t[:, wB:wB + 256],
                                           wob[:, hk, 128 * c:128 * c + 128],
                                           src[0:64, hk, 0:256],
                                           start=(hk == 0), stop=(hk == 15))
                        cPE5.add(ins, 1)

                @AC
                def _(e, bp=bp):
                    for c in range(8):
                        e.wait_ge(cPE5.h, 8 * bp + c + 1)
                        wB = 512 * (c % 2)
                        cEV5.add(e.activation(aoT[:, c, 256 * bp:256 * bp + 256],
                                              p_wo_t[:, wB:wB + 256], AF.Identity,
                                              bias=t_boT[:, c:c + 1]), 1)

        if PH < 6:
            _emit_taps()
            return nc
        with nc.psum_tensor("pPr", [128, 8 * 512], F32) as p_pr:
            @SY
            def _(e):
                e.wait_ge(cEV5.h, 32)
                e.wait_ge(cOT.h, 32)
                sPB32.add(e.dma_start(out=pbf32, in_=d_pb), 16)
                e.wait_ge(cPB.h, 2)
                sX6A.add(e.dma_start(out=x6[0], in_=d_x[1:129, :]), 16)
                sX6B.add(e.dma_start(out=x6[1], in_=d_x[129:257, :]), 16)

            @AC
            def _(e):
                e.wait_ge(sPB32.h, 16)
                e.wait_ge(cEV5.h, 32)
                cPB.add(e.copy(pbhi[0:1, :], pbf32[0:1, :]), 1)

            @DV
            def _(e):
                e.wait_ge(cPB.h, 1)
                cPB.add(e.tensor_sub(pblo[0:1, :], pbf32[0:1, :], pbhi[0:1, :]), 1)

            for tt in range(8):
                @PE
                def _(e, tt=tt):
                    if tt == 0:
                        e.wait_ge(cEV5.h, 32)
                        e.wait_ge(sCST2.h, 48)
                        e.wait_ge(sPW.h, 16)
                        e.wait_ge(cPB.h, 2)
                        e.wait_ge(cPTe.h, 32)
                        e.wait_ge(cPS.h, 128)
                        e.wait_ge(cOTe.h, 128)
                    for seg in range(8):
                        if tt >= 1:
                            e.wait_ge(cGI.h, 8 * (tt - 1) + seg + 1)
                        for kc in range(8):
                            e.matmul(p_pr[:, 512 * seg:512 * seg + 512],
                                     aoT[:, kc, 128 * tt:128 * tt + 128],
                                     pwv[:, kc, 512 * seg:512 * seg + 512],
                                     start=(kc == 0), stop=False)
                        e.matmul(p_pr[:, 512 * seg:512 * seg + 512],
                                 t_ones2[0:1, :],
                                 pbhi[0:1, 512 * seg:512 * seg + 512],
                                 start=False, stop=False)
                        cPE6.add(e.matmul(p_pr[:, 512 * seg:512 * seg + 512],
                                          t_ones2[0:1, :],
                                          pblo[0:1, 512 * seg:512 * seg + 512],
                                          start=False, stop=True), 1)

                @DV
                def _(e, tt=tt):
                    ins = e.tensor_tensor(t_sc3[:], t_r[:, tt:tt + 1],
                                          t_wsum[:, tt:tt + 1], ALU.mult)
                    cSC3.add(ins, 1)
                    if tt >= 2:
                        e.wait_ge(sSTs[tt % 2].h, 16 * (tt // 2))
                    for seg in range(8):
                        e.wait_ge(cPE6.h, 16 * tt + seg + 1)
                        cSev.add(e.tensor_copy(s6[tt % 2][:, 512 * seg:512 * seg + 512],
                                               p_pr[:, 512 * seg:512 * seg + 512]), 1)

                @PE
                def _(e, tt=tt):
                    if tt == 0:
                        e.wait_ge(sEB.h, 16)
                    for seg in range(8):
                        e.wait_ge(cSev.h, 8 * tt + seg + 1)
                        cPE6.add(e.matmul(p_pr[:, 512 * seg:512 * seg + 512],
                                          t_wmT[0:64, 128 * tt:128 * tt + 128],
                                          ebv[0:64, 512 * seg:512 * seg + 512],
                                          start=True, stop=True), 1)

                @DV
                def _(e, tt=tt):
                    e.wait_ge(cSC3.h, tt + 1)
                    if tt >= 1:
                        e.wait_ge(cPL.h, 2 * tt)
                    e.wait_ge(sX6[tt % 2].h, 16 * (tt // 2 + 1))
                    if tt == 0:
                        e.wait_ge(cEV5.h, 32)
                    dedge(e, e.tensor_scalar(gw, x6[tt % 2],
                                             t_mrv[:, 2 * tt:2 * tt + 1],
                                             t_sc3[:], ALU.subtract, ALU.mult))
                    for seg in range(8):
                        e.wait_ge(cPE6.h, 16 * tt + 8 + seg + 1)
                        cGI.add(e.tensor_add(gw[:, 512 * seg:512 * seg + 512],
                                             gw[:, 512 * seg:512 * seg + 512],
                                             p_pr[:, 512 * seg:512 * seg + 512]), 1)
                    if tt >= 1:
                        e.wait_ge(cPL.h, 2 * tt)
                        cFIN.add(e.tensor_add(s6[(tt - 1) % 2], s6[(tt - 1) % 2],
                                              x6[(tt - 1) % 2]), 2)

                @AC
                def _(e, tt=tt):
                    e.wait_ge(cGI.h, 8 * (tt + 1))
                    cSIL.add(e.activation(gw, gw, AF.Silu), 1)

                @GP
                def _(e, tt=tt):
                    e.wait_ge(cSIL.h, tt + 1)
                    cPL.add(e.tensor_mul(s6[tt % 2][:, 0:2048], s6[tt % 2][:, 0:2048],
                                         gw[:, 0:2048]), 1)

                @DV
                def _(e, tt=tt):
                    e.wait_ge(cSIL.h, tt + 1)
                    dedge(e, e.tensor_mul(s6[tt % 2][:, 2048:4096],
                                          s6[tt % 2][:, 2048:4096],
                                          gw[:, 2048:4096]))
                    cPL.add(e.engine_nop(), 1)

                @SY
                def _(e, tt=tt):
                    if tt >= 1:
                        e.wait_ge(cFIN.h, 2 * tt)
                        sSTs[(tt - 1) % 2].add(
                            e.dma_start(out=d_out[128 * (tt - 1):128 * tt, :],
                                        in_=s6[(tt - 1) % 2]), 16)
                        if tt + 1 < 8:
                            r0 = 1 + 128 * (tt + 1)
                            sX6[(tt + 1) % 2].add(
                                e.dma_start(out=x6[(tt + 1) % 2],
                                            in_=d_x[r0:r0 + 128, :]), 16)

            @DV
            def _(e):
                e.wait_ge(cPL.h, 16)
                cFIN.add(e.tensor_add(s6[7 % 2], s6[7 % 2], x6[7 % 2]), 2)

            @SY
            def _(e):
                e.wait_ge(cFIN.h, 16)
                sSTs[7 % 2].add(e.dma_start(out=d_out[128 * 7:128 * 8, :],
                                            in_=s6[7 % 2]), 16)
                e.wait_ge(sSTa.h, 16 * 4)
                e.wait_ge(sSTb.h, 16 * 4)

        _emit_taps()
    return nc



def _prep(inputs):
    f32 = np.float32
    x = np.asarray(inputs["L0_attn_inputs"], f32).reshape(8192, D)
    conv_w = np.asarray(inputs["L0_attn_conv_w"], f32)
    cw = np.ascontiguousarray(
        conv_w.reshape(3, 32, 128, NSUP, 256).transpose(3, 0, 1, 2, 4)
        .reshape(NSUP, NGRP, 8, 128, 256))
    wq = np.ascontiguousarray(np.asarray(inputs["L0_attn_wq"], f32).reshape(1024, 1024))
    wk = np.ascontiguousarray(np.asarray(inputs["L0_attn_wk"], f32).reshape(1024, 1024))
    wv = np.ascontiguousarray(np.asarray(inputs["L0_attn_wv"], f32).reshape(1024, 1024))
    wo = np.ascontiguousarray(np.asarray(inputs["L0_attn_wo"], f32).reshape(1024, 1024))
    pwm = np.ascontiguousarray(np.asarray(inputs["L0_attn_proj_w"], f32))
    rw = np.ascontiguousarray(np.asarray(inputs["L0_ffn_router_weight"], f32))
    eb = np.ascontiguousarray(np.asarray(inputs["L0_ffn_expert_bias"], f32))
    cbT = np.ascontiguousarray(np.asarray(inputs["L0_attn_conv_b"], f32).reshape(24, 128).T)
    bqT = np.ascontiguousarray(np.asarray(inputs["L0_attn_bq"], f32).reshape(8, 128).T)
    bkT = np.ascontiguousarray(np.asarray(inputs["L0_attn_bk"], f32).reshape(8, 128).T)
    bvT = np.ascontiguousarray(np.asarray(inputs["L0_attn_bv"], f32).reshape(16, 64).T)
    boT = np.ascontiguousarray(np.asarray(inputs["L0_attn_bo"], f32).reshape(8, 128).T)
    pb = np.ascontiguousarray(np.asarray(inputs["L0_attn_proj_b"], f32).reshape(1, 4096))
    rbT = np.ascontiguousarray(np.asarray(inputs["L0_ffn_router_bias"], f32).reshape(64, 1))

    i128 = np.eye(128, dtype=f32)
    i64 = np.eye(64, dtype=f32)
    tril = np.tril(np.ones((128, 128), bool))
    m1 = np.where(tril, np.float32(0.0), np.float32(-1e9))
    mask4 = np.ascontiguousarray(np.concatenate([m1] * 4, axis=1))
    ones2 = np.ones((2, 128), dtype=f32)

    common = dict(cw=cw, wq=wq, wk=wk, wv=wv, wo=wo, pw=pwm, rw=rw, eb=eb,
                  cbT=cbT, bqT=bqT, bkT=bkT, bvT=bvT, boT=boT, pb=pb, rbT=rbT,
                  i128=i128, i64=i64, mask4=mask4, ones2=ones2)

    in_maps = []
    zrow = np.zeros((1, D), f32)
    for c in range(8):
        lo = c * 1024
        left = zrow if (lo % 4096) == 0 else x[lo - 1:lo]
        right = zrow if ((lo + 1024) % 4096) == 0 else x[lo + 1024:lo + 1025]
        shard = np.ascontiguousarray(np.concatenate([left, x[lo:lo + 1024], right], 0))
        m = dict(common)
        m["x"] = shard
        in_maps.append(m)
    return in_maps


def kernel(**inputs):
    in_maps = _prep(inputs)
    nc = build_program()
    res = run_bass_kernel_spmd(nc, in_maps, core_ids=list(range(8)))
    out = np.concatenate([r["out"] for r in res.results], axis=0)
    return out.reshape(2, 4096, 4096)


if __name__ == "__main__":
    nc = build_program()
    print("instructions:", nc.n_instructions())
